# version 15
# baseline (speedup 1.0000x reference)
import math
from contextlib import ExitStack

import numpy as np
import ml_dtypes
import concourse.bass as bass
import concourse.mybir as mybir
from concourse.bass_utils import run_bass_kernel_spmd

F32 = mybir.dt.float32
BF16 = mybir.dt.bfloat16
AF = mybir.ActivationFunctionType
ALU = mybir.AluOpType
AX = mybir.AxisListType
BF = ml_dtypes.bfloat16

C = 128
NPRE = 5
NOWN = 32
NCH = NPRE + NOWN
TOK = NCH * C
HID = 1024
HK, HV = 4, 8
DK = DV = 128
KEY, VAL, CONVD, KC = 512, 1024, 2048, 4
NEXP, INTER = 8, 512
EPS = 1e-6
NEGB = -1.0e30
LNEPS = 1e-5
LVLS = [1, 2, 4, 8, 16, 32, 64]
LSQ = 5


class Ctr:
    def __init__(self, nc, name):
        self.h = nc.alloc_semaphore(name)
        self.n = 0

    def inc(self, inst, k=1):
        inst.then_inc(self.h, k)
        self.n += k
        return inst


class _VG:

    WIN = 3

    def __init__(self, v, jnk, hist):
        self._v = v
        self._jnk = jnk
        self._hist = hist

    @staticmethod
    def _span(ap):
        if ap is None or isinstance(ap, (int, float)):
            return None
        try:
            off = ap.offset
            sh = ap.shape
            st = ap.stride
            ext = 1
            for i in range(1, len(sh)):
                ext += (sh[i] - 1) * abs(st[i])
            return (ap.name, off, off + ext)
        except Exception:
            try:
                return (ap.name, 0, 1 << 40)
            except Exception:
                return ("?", 0, 1 << 40)

    def _op(self, writes, reads, fn):
        rs = [r for r in (self._span(a) for a in reads) if r is not None]

        def hit(ws):
            for w in ws:
                for r in rs:
                    if w[0] == r[0] and w[1] < r[2] and r[1] < w[2]:
                        return True
            return False

        h = self._hist
        while any(hit(ws) for ws in h):
            self._v.memset(self._jnk, 0.0)
            h.append([])
            del h[:-self.WIN]
        inst = fn()
        h.append([w for w in (self._span(a) for a in writes) if w is not None])
        del h[:-self.WIN]
        return inst

    def wait_ge(self, *a, **k):
        return self._v.wait_ge(*a, **k)

    def memset(self, ap, cval):
        return self._op([ap], [], lambda: self._v.memset(ap, cval))

    def tensor_copy(self, out, in_):
        return self._op([out], [in_], lambda: self._v.tensor_copy(out, in_))

    def tensor_tensor(self, out, in0, in1, op):
        return self._op([out], [in0, in1],
                        lambda: self._v.tensor_tensor(out, in0, in1, op))

    def tensor_mul(self, out, in0, in1):
        return self._op([out], [in0, in1],
                        lambda: self._v.tensor_mul(out, in0, in1))

    def tensor_scalar(self, out, in0, s1, s2, op0, op1=mybir.AluOpType.bypass,
                      accum_out=None):
        return self._op([out, accum_out], [in0, s1, s2],
                        lambda: self._v.tensor_scalar(out, in0, s1, s2, op0,
                                                      op1,
                                                      accum_out=accum_out))

    def tensor_scalar_mul(self, out, in0, s1):
        return self._op([out], [in0, s1],
                        lambda: self._v.tensor_scalar_mul(out, in0, s1))

    def tensor_scalar_add(self, out, in0, s1):
        return self._op([out], [in0, s1],
                        lambda: self._v.tensor_scalar_add(out, in0, s1))

    def tensor_scalar_sub(self, out, in0, s1):
        return self._op([out], [in0, s1],
                        lambda: self._v.tensor_scalar_sub(out, in0, s1))

    def tensor_scalar_max(self, out, in0, s1):
        return self._op([out], [in0, s1],
                        lambda: self._v.tensor_scalar_max(out, in0, s1))

    def scalar_tensor_tensor(self, out, in0, scalar, in1, op0, op1):
        return self._op([out], [in0, scalar, in1],
                        lambda: self._v.scalar_tensor_tensor(out, in0, scalar,
                                                             in1, op0, op1))

    def reduce_sum(self, out, in_, axis, **kw):
        return self._op([out], [in_],
                        lambda: self._v.reduce_sum(out, in_, axis, **kw))

    def reduce_max(self, out, in_, axis, **kw):
        return self._op([out], [in_],
                        lambda: self._v.reduce_max(out, in_, axis, **kw))

    def reciprocal(self, out, in_):
        return self._op([out], [in_], lambda: self._v.reciprocal(out, in_))


def _consts():
    i_bf = np.eye(C, dtype=BF)
    i_f = np.eye(C, dtype=np.float32)
    ones1 = np.ones((1, C), dtype=np.float32)
    ut = np.triu(np.ones((C, C), dtype=np.float32))
    m_up1 = np.where(np.triu(np.ones((C, C), bool), 1), 0.0, NEGB)
    m_lo1 = np.where(np.tril(np.ones((C, C), bool), -1), 0.0, NEGB)
    m_lo0 = np.where(np.tril(np.ones((C, C), bool), 0), 0.0, NEGB)
    masks = np.concatenate([m_lo1, m_up1, m_lo0], 1).astype(np.float32)
    r127 = np.ones((C, C), dtype=np.float32)
    shifts = np.zeros((C, KC * C), dtype=BF)
    for j in range(KC):
        for t in range(C):
            tp = t - (3 - j)
            if 0 <= tp < C:
                shifts[tp, j * C + t] = 1
    halos = np.zeros((C, 3 * C), dtype=BF)
    for j in range(3):
        for t in range(C):
            tp = t + 125 + j
            if tp < C:
                halos[tp, j * C + t] = 1
    return i_bf, i_f, ones1, ut, masks, shifts, halos, r127


def build_program(dbg=False):
    nc = bass.Bass("TRN2", target_bir_lowering=False,
                   detect_race_conditions=False)

    x_h = nc.dram_tensor("x", [TOK, HID], F32, kind="ExternalInput")
    out_h = nc.dram_tensor("out", [NOWN * C, HID], F32, kind="ExternalOutput")
    Dd = lambda n, s: nc.dram_tensor(n, s, F32, kind="ExternalInput")
    w_ln1 = Dd("w_ln1", [HID]); w_ln2 = Dd("w_ln2", [HID])
    W_qkvz = Dd("W_qkvz", [HID, 3072]); W_ba = Dd("W_ba", [HID, 16])
    conv_w = Dd("conv_w", [CONVD, KC]); dt_bias = Dd("dt_bias", [HV])
    A_log = Dd("A_log", [HV]); gnw = Dd("gate_norm_w", [DV])
    W_out = Dd("W_out", [VAL, HID]); W_router = Dd("W_router", [HID, NEXP])
    Wge = Dd("W_gate_e", [NEXP, HID, INTER]); Wue = Dd("W_up_e", [NEXP, HID, INTER])
    Wde = Dd("W_down_e", [NEXP, INTER, HID]); fgw = Dd("fgw", [HID, NEXP])
    w1 = Dd("w1", [NEXP, HID, INTER]); w2 = Dd("w2", [NEXP, INTER, HID])
    w3 = Dd("w3", [NEXP, HID, INTER])

    spk = "ExternalOutput" if dbg else "Internal"
    sp_qkvs = nc.dram_tensor("sp_qkvs", [NCH, C, CONVD], BF16, kind=spk)
    sp_sz = nc.dram_tensor("sp_sz", [NCH, C, VAL], BF16, kind=spk)
    sp_xnrfm = nc.dram_tensor("sp_xnrfm", [NCH, C, HID], BF16, kind=spk)
    sp_x2rfm = nc.dram_tensor("sp_x2rfm", [NOWN, C, HID], BF16, kind=spk)
    sp_outp = nc.dram_tensor("sp_outp", [NOWN, C, HID], F32, kind=spk)
    sp_otile = nc.dram_tensor("sp_otile", [NOWN, C, VAL], F32,
                              kind="ExternalOutput" if dbg else "Internal")
    sp_dbgf = nc.dram_tensor("sp_dbgf", [C, 256], F32,
                             kind="ExternalOutput" if dbg else "Internal")
    sp_dbgb = nc.dram_tensor("sp_dbgb", [C, 2048], BF16,
                             kind="ExternalOutput" if dbg else "Internal")
    l_gu = nc.dram_tensor("l_gu", [4, NEXP, C, 8, INTER], BF16, kind="Internal")
    l_dn = nc.dram_tensor("l_dn", [2, NEXP, C, 4, HID], BF16, kind="Internal")

    (i_bf_np, i_f_np, ones1_np, ut_np, masks_np, shifts_np, halos_np,
     r127_np) = _consts()
    c_ibf = nc.inline_tensor(i_bf_np, "c_ibf")
    c_if = nc.inline_tensor(i_f_np, "c_if")
    c_ones1 = nc.inline_tensor(ones1_np, "c_ones1")
    c_ut = nc.inline_tensor(ut_np, "c_ut")
    c_masks = nc.inline_tensor(masks_np, "c_masks")
    c_shift = nc.inline_tensor(shifts_np, "c_shift")
    c_halo = nc.inline_tensor(halos_np, "c_halo")
    c_r127 = nc.inline_tensor(r127_np, "c_r127")

    A = nc.alloc_sbuf_tensor
    ibf = A("ibf", [C, C], BF16)
    if32 = A("if32", [C, C], F32)
    ones1 = A("ones1", [1, C], F32)
    utf = A("utf", [C, C], F32)
    masks = A("masks", [C, 3 * C], F32)
    shifts = A("shifts", [C, KC * C], BF16)
    halos = A("halos", [C, 3 * C], BF16)
    r127f = A("r127f", [C, C], F32)
    col_m = A("col_m", [C, NCH], F32)
    col_rstd = A("col_rstd", [C, NCH], F32)
    col_rr = A("col_rr", [C, NCH], F32)
    col_ir1 = A("col_ir1", [C, NCH], F32)
    col_ba = A("col_ba", [C, NCH * 16], F32)
    col_ssq = A("col_ssq", [C, NCH * 8], F32)
    col_fflog = A("col_fflog", [C, NCH * 8], F32)
    col_route = A("col_route", [C, NOWN * 8], F32)
    col_sffn = A("col_sffn", [C, NOWN * 8], F32)
    ln12 = A("ln12", [C, 16], F32)
    gnw_c = A("gnw_c", [C, 1], F32)
    expA_b = A("expA_b", [C, 8], F32)
    dt_b = A("dt_b", [C, 8], F32)
    wba_bf = A("wba_bf", [C, 8 * 16], BF16)
    wrt_bf = A("wrt_bf", [C, 8 * NEXP], BF16)
    fgw_bf = A("fgw_bf", [C, 8 * NEXP], BF16)
    s_state = A("s_state", [C, 2 * HV * DV], BF16)
    rows0 = A("rows0", [1, 16 + CONVD], F32)
    cst = A("cst", [C, 12], F32)
    vjnk = A("vjnk", [C, 8], F32)
    CVALS = [LNEPS, 1e-11, EPS, 1.0 / HID, (1.0 + EPS) / HID, -0.5, 1.0 / DV,
             -0.5 * math.log(128.0), NEGB, -1.0, 0.0, 1.0]
    for ci, cv in enumerate(CVALS):
        nc.const_aps.register(F32, cv, cst[:, ci:ci + 1])

    sL = Ctr(nc, "sL"); sG = Ctr(nc, "sG"); sT = Ctr(nc, "sT")
    sV = Ctr(nc, "sV"); sS = Ctr(nc, "sS"); sO = Ctr(nc, "sO")
    sn = {}

    def phase_gate(blk_):
        gates = [(sL, "sync"), (sG, "gpsimd"), (sT, "tensor"),
                 (sV, "vector"), (sS, "scalar"), (sO, "stores")]
        vals = {id(ct): ct.n for ct, _ in gates}

        def mk(eng_name):
            def f(eng):
                for ct, nm in gates:
                    if nm != eng_name and vals[id(ct)] > 0:
                        eng.wait_ge(ct.h, vals[id(ct)])
            return f
        blk_.sync(mk("sync"))
        blk_.gpsimd(mk("gpsimd"))
        blk_.tensor(mk("tensor"))
        blk_.vector(mk("vector"))
        blk_.scalar(mk("scalar"))

    est = ExitStack()
    blk = est.enter_context(nc.Block())

    _vhist = []

    def vec(fn):
        return blk.vector(lambda v: fn(_VG(v, vjnk[:, 0:8], _vhist)))
    p01s = ExitStack()
    wq_bf = p01s.enter_context(nc.sbuf_tensor("wq_bf", [C, 8 * 3072], BF16))
    wjb = p01s.enter_context(nc.sbuf_tensor("wjb", [C, KC * CONVD], BF16))
    b0a = p01s.enter_context(nc.sbuf_tensor("p0_b1", [C, 2048], BF16))
    b0b = p01s.enter_context(nc.sbuf_tensor("p0_b2", [C, 2048], BF16))

    p0s = ExitStack()
    stg = p0s.enter_context(nc.sbuf_tensor("p0_stg", [C, 3072], F32))
    p0ps = p0s.enter_context(nc.psum_tensor("p0_ps", [C, 512], F32))

    @blk.sync
    def _(s):
        sL.inc(s.dma_start(out=ibf[:], in_=c_ibf[:, :]), 16)
        sL.inc(s.dma_start(out=if32[:], in_=c_if[:, :]), 16)
        sL.inc(s.dma_start(out=ones1[:], in_=c_ones1[:, :]), 16)
        sL.inc(s.dma_start(out=utf[:], in_=c_ut[:, :]), 16)
        sL.inc(s.dma_start(out=masks[:], in_=c_masks[:, :]), 16)
        sL.inc(s.dma_start(out=shifts[:], in_=c_shift[:, :]), 16)
        sL.inc(s.dma_start(out=halos[:], in_=c_halo[:, :]), 16)
        sL.inc(s.dma_start(out=r127f[:], in_=c_r127[:, :]), 16)
        sL.inc(s.dma_start(out=ln12[:, 0:8],
                           in_=w_ln1.ap().rearrange("(t p) -> p t", p=C),
                           allow_slow_non_contiguous=True), 16)
        sL.inc(s.dma_start(out=ln12[:, 8:16],
                           in_=w_ln2.ap().rearrange("(t p) -> p t", p=C),
                           allow_slow_non_contiguous=True), 16)
        sL.inc(s.dma_start(out=gnw_c[:, 0:1],
                           in_=gnw.ap()[:, None],
                           allow_slow_non_contiguous=True), 16)
        sL.inc(s.dma_start(out=rows0[0:1, 0:8], in_=dt_bias.ap()[None, :]), 16)
        sL.inc(s.dma_start(out=rows0[0:1, 8:16], in_=A_log.ap()[None, :]), 16)

    ld_consts = sL.n

    @blk.gpsimd
    def _(g):
        for d in range(8):
            sG.inc(g.dma_start(out=wba_bf[:, d * 16:(d + 1) * 16],
                               in_=W_ba.ap()[d * C:(d + 1) * C, :]), 16)
            sG.inc(g.dma_start(out=wrt_bf[:, d * 8:(d + 1) * 8],
                               in_=W_router.ap()[d * C:(d + 1) * C, :]), 16)
            sG.inc(g.dma_start(out=fgw_bf[:, d * 8:(d + 1) * 8],
                               in_=fgw.ap()[d * C:(d + 1) * C, :]), 16)
    gl_small = sG.n

    @vec
    def _(v):
        v.wait_ge(sL.h, ld_consts)
        for ci, cv in enumerate(CVALS):
            v.memset(cst[:, ci:ci + 1], cv)
        v.memset(s_state[:], 0.0)
        sV.inc(v.tensor_scalar_add(ln12[:], ln12[:], cst[:, 11:12]))
    v_ln = sV.n

    @blk.tensor
    def _(t):
        t.wait_ge(sL.h, ld_consts)
        t.matmul(p0ps[:, 0:8], ones1[0:1, :], rows0[0:1, 8:16], start=True, stop=False)
        sT.inc(t.matmul(p0ps[:, 8:16], ones1[0:1, :], rows0[0:1, 0:8],
                        start=False, stop=True))
    t_adt = sT.n

    @blk.scalar
    def _(sc):
        sc.wait_ge(sT.h, t_adt)
        sc.wait_ge(sV.h, v_ln)
        sc.activation(expA_b[:], p0ps[:, 0:8], AF.Exp)
        sS.inc(sc.copy(dt_b[:], p0ps[:, 8:16]))

    for j in range(KC):
        @blk.sync
        def _(s, j=j):
            if j >= 1:
                s.wait_ge(sT.h, sT.n)
            sL.inc(s.dma_start(out=rows0[0:1, 16:16 + CONVD],
                               in_=conv_w.ap()[:, j][None, :],
                               allow_slow_non_contiguous=True), 16)
        for gq in range(4):
            @blk.tensor
            def _(t, j=j, gq=gq):
                t.wait_ge(sL.h, sL.n)
                t.wait_ge(sS.h, sS.n)
                sT.inc(t.matmul(p0ps[:], ones1[0:1, :],
                                rows0[0:1, 16 + gq * 512:16 + (gq + 1) * 512],
                                start=True, stop=True))

            @blk.scalar
            def _(sc, j=j, gq=gq):
                sc.wait_ge(sT.h, sT.n)
                sS.inc(sc.copy(wjb[:, j * CONVD + gq * 512:j * CONVD + (gq + 1) * 512],
                               p0ps[:]))

    for d in range(8):
        @blk.sync
        def _(s, d=d):
            if d >= 1:
                s.wait_ge(sV.h, v_ln + d)
            sL.inc(s.dma_start(out=stg[:], in_=W_qkvz.ap()[d * C:(d + 1) * C, :]), 16)

        @vec
        def _(v, d=d):
            v.wait_ge(sL.h, sL.n)
            sV.inc(v.tensor_scalar_mul(wq_bf[:, d * 3072:(d + 1) * 3072],
                                       stg[:], ln12[:, d:d + 1]))
    v_wq = sV.n

    for d in range(8):
        @vec
        def _(v, d=d):
            if d == 0:
                v.wait_ge(sG.h, gl_small)
            sV.inc(v.tensor_scalar_mul(wba_bf[:, d * 16:(d + 1) * 16],
                                       wba_bf[:, d * 16:(d + 1) * 16],
                                       ln12[:, d:d + 1]))
    v_wba = sV.n
    p0s.close()

    moe_jobs = []
    for ti, th in enumerate([Wge, Wue, w1, w3]):
        for e in range(NEXP):
            for dh in range(2):
                moe_jobs.append(("gu", ti, e, dh, th))
    for ti, th in enumerate([Wde, w2]):
        for e in range(NEXP):
            for dh in range(2):
                moe_jobs.append(("dn", ti, e, dh, th))
    mst = {"i": 0, "bufs": [b0a, b0b], "st": [None, None], "final": None}

    def emit_moe_jobs(k):
        for _ in range(k):
            i = mst["i"]
            if i >= len(moe_jobs):
                mst["final"] = mst["final"] or sO.n
                return
            mst["i"] = i + 1
            kind, ti, e, dh, th = moe_jobs[i]
            buf = mst["bufs"][i % 2]
            prev_st = mst["st"][i % 2]

            @blk.gpsimd
            def _(g, kind=kind, ti=ti, e=e, dh=dh, th=th, buf=buf, prev_st=prev_st):
                if prev_st is not None:
                    g.wait_ge(sO.h, prev_st)
                if kind == "gu":
                    sG.inc(g.dma_start(
                        out=buf[:].rearrange("p (c f) -> p c f", c=4),
                        in_=th.ap()[e, dh * 512:(dh + 1) * 512, :]
                            .rearrange("(c p) f -> p c f", p=C)), 16)
                else:
                    sG.inc(g.dma_start(
                        out=buf[:].rearrange("p (c f) -> p c f", c=2),
                        in_=th.ap()[e, dh * 256:(dh + 1) * 256, :]
                            .rearrange("(c p) f -> p c f", p=C)), 16)
            gl = sG.n

            @blk.sync
            def _(s, kind=kind, ti=ti, e=e, dh=dh, buf=buf, gl=gl):
                s.wait_ge(sG.h, gl)
                if kind == "gu":
                    sO.inc(s.dma_start(
                        out=l_gu.ap()[ti, e, :, dh * 4:(dh + 1) * 4, :],
                        in_=buf[:].rearrange("p (c f) -> p c f", c=4)), 16)
                else:
                    sO.inc(s.dma_start(
                        out=l_dn.ap()[ti, e, :, dh * 2:(dh + 1) * 2, :],
                        in_=buf[:].rearrange("p (c f) -> p c f", c=2)), 16)
            mst["st"][i % 2] = sO.n

    phase_gate(blk)
    p1as = ExitStack()
    xa_t = p1as.enter_context(nc.sbuf_tensor("xa", [C, 2 * HID], F32))
    xa = lambda i: xa_t[:, i * HID:(i + 1) * HID]
    scr1 = p1as.enter_context(nc.sbuf_tensor("p1a_scr", [C, HID], F32))
    ccol = p1as.enter_context(nc.sbuf_tensor("p1a_col", [C, 8], F32))

    for c in range(NCH):
        @blk.sync
        def _(s, c=c):
            if c >= 2:
                s.wait_ge(sV.h, sn[("a_rel", c - 2)])
            sL.inc(s.dma_start(out=xa(c % 2), in_=x_h.ap()[c * C:(c + 1) * C, :]), 16)

        @vec
        def _(v, c=c):
            v.wait_ge(sL.h, sL.n)
            if c == 0:
                v.wait_ge(sV.h, v_wba)
            if c >= 1:
                v.wait_ge(sS.h, sS.n)
            v.reduce_sum(ccol[:, 0:1], xa(c % 2), axis=AX.X)
            v.tensor_scalar_mul(col_m[:, c:c + 1], ccol[:, 0:1], cst[:, 3:4])
            sV.inc(v.tensor_scalar_sub(scr1[:], xa(c % 2), col_m[:, c:c + 1]))
            sn[("a_rel", c)] = sV.n

        @blk.scalar
        def _(sc, c=c):
            sc.wait_ge(sV.h, sV.n)
            sc.activation(scr1[:], scr1[:], AF.Square, accum_out=ccol[:, 1:2])
            sc.copy(ccol[:, 6:7], ccol[:, 0:1])
            sc.activation(ccol[:, 2:3], ccol[:, 1:2], AF.Ln, bias=LNEPS, scale=1.0 / HID)
            sc.activation(ccol[:, 3:4], ccol[:, 1:2], AF.Ln,
                          bias=1e-11, scale=(1.0 + EPS) / HID)
            sc.activation(col_rstd[:, c:c + 1], ccol[:, 2:3], AF.Exp, scale=-0.5)
            sS.inc(sc.activation(col_rr[:, c:c + 1], ccol[:, 3:4], AF.Exp, scale=-0.5))

        @vec
        def _(v, c=c):
            v.wait_ge(sS.h, sS.n)
            v.reciprocal(ccol[:, 4:5], col_rr[:, c:c + 1])
            v.tensor_copy(ccol[:, 5:6], col_rstd[:, c:c + 1])
            sV.inc(v.tensor_mul(col_ir1[:, c:c + 1], ccol[:, 4:5], ccol[:, 5:6]))
    p1as.close()

    phase_gate(blk)
    p1bs = ExitStack()
    xb1 = p1bs.enter_context(nc.sbuf_tensor("xb1", [C, HID], F32))
    xnr2 = p1bs.enter_context(nc.sbuf_tensor("xnr2", [C, HID], BF16))
    xnrfm_t = p1bs.enter_context(nc.sbuf_tensor("xnrfm", [C, 2 * HID], BF16))
    xnrfm = lambda i: xnrfm_t[:, i * HID:(i + 1) * HID]
    pre_t = p1bs.enter_context(nc.sbuf_tensor("pre", [C, 2 * KC * CONVD], BF16))
    pre = lambda s_, j_: pre_t[:, (s_ * KC + j_) * CONVD:(s_ * KC + j_ + 1) * CONVD]
    qkvs_t = p1bs.enter_context(nc.sbuf_tensor("qkvs_t", [C, CONVD], BF16))
    sz_t = p1bs.enter_context(nc.sbuf_tensor("sz_t", [C, VAL], BF16))
    l2scr = p1bs.enter_context(nc.sbuf_tensor("l2scr", [C, C], F32))
    pq0 = p1bs.enter_context(nc.psum_tensor("pq0", [C, 512], F32))
    pq1 = p1bs.enter_context(nc.psum_tensor("pq1", [C, 512], F32))
    pc0 = p1bs.enter_context(nc.psum_tensor("pc0", [C, 512], F32))
    pc1 = p1bs.enter_context(nc.psum_tensor("pc1", [C, 512], F32))
    pt1b = p1bs.enter_context(nc.psum_tensor("pt1b", [C, HID], BF16))
    pl1b = p1bs.enter_context(nc.psum_tensor("pl1b", [C, 32], F32))
    pq = [pq0, pq1]
    pcv = [pc0, pc1]

    for c in range(NCH):
        emit_moe_jobs(3)

        @blk.sync
        def _(s, c=c):
            if c >= 1:
                s.wait_ge(sV.h, sn[("b_xrel", c - 1)])
            if c == 0:
                s.wait_ge(sV.h, v_wq)
            sL.inc(s.dma_start(out=xb1[:], in_=x_h.ap()[c * C:(c + 1) * C, :]), 16)

        @vec
        def _(v, c=c):
            v.wait_ge(sL.h, sL.n)
            if c >= 1:
                _sem, _val = sn[("b_fmdone", c - 1)]
                v.wait_ge(_sem, _val)
            sV.inc(v.tensor_scalar(xnr2[:], xb1[:],
                                   col_m[:, c:c + 1], col_rr[:, c:c + 1],
                                   ALU.subtract, ALU.mult))
            sn[("b_xrel", c)] = sV.n

        @blk.tensor
        def _(t, c=c):
            t.wait_ge(sV.h, sV.n)
            if c >= 1:
                _sem, _val = sn[("b_fmdone", c - 1)]
                t.wait_ge(_sem, _val)
            mi = None
            for k in range(8):
                mi = t.transpose(pt1b[:, k * C:(k + 1) * C],
                                 xnr2[:, k * C:(k + 1) * C], ibf[:])
            sT.inc(mi)
            sn[("b_tr", c)] = sT.n

        @vec
        def _(v, c=c):
            v.wait_ge(sT.h, sn[("b_tr", c)])
            if c >= 2:
                v.wait_ge(sL.h, sn[("b_fmsp", c - 2)])
            sV.inc(v.tensor_copy(xnrfm(c % 2), pt1b[:]))
            sn[("b_fmdone", c)] = (sV.h, sV.n)

        @blk.sync
        def _(s, c=c):
            sem, val = sn[("b_fmdone", c)]
            s.wait_ge(sem, val)
            sL.inc(s.dma_start(out=sp_xnrfm.ap()[c], in_=xnrfm(c % 2)), 16)
            sn[("b_fmsp", c)] = sL.n

        for fg in range(6):
            @blk.tensor
            def _(t, c=c, fg=fg):
                if fg == 0:
                    sem, val = sn[("b_fmdone", c)]
                    t.wait_ge(sem, val)
                key = ("b_qd", c, fg - 2) if fg >= 2 else ("b_qd", c - 1, fg + 4)
                if key in sn:
                    sem, val = sn[key]
                    t.wait_ge(sem, val)
                mi = None
                for d in range(8):
                    mi = t.matmul(pq[fg % 2][:],
                                  xnrfm(c % 2)[:, d * C:(d + 1) * C],
                                  wq_bf[:, d * 3072 + fg * 512:d * 3072 + (fg + 1) * 512],
                                  start=(d == 0), stop=(d == 7))
                sT.inc(mi)
                sn[("b_qmm", c, fg)] = sT.n

            if fg < 4:
                @vec
                def _(v, c=c, fg=fg):
                    v.wait_ge(sT.h, sn[("b_qmm", c, fg)])
                    if fg == 0 and ("b_convdone", c - 1) in sn:
                        v.wait_ge(sT.h, sn[("b_convdone", c - 1)])
                    mi = None
                    for j in range(KC):
                        mi = v.tensor_tensor(
                            pre(c % 2, j)[:, fg * 512:(fg + 1) * 512],
                            pq[fg % 2][:],
                            wjb[:, j * CONVD + fg * 512:j * CONVD + (fg + 1) * 512],
                            ALU.mult)
                    sV.inc(mi)
                    sn[("b_qd", c, fg)] = (sV.h, sV.n)
            else:
                @blk.scalar
                def _(sc, c=c, fg=fg):
                    sc.wait_ge(sT.h, sn[("b_qmm", c, fg)])
                    if c >= 1 and fg == 4:
                        sc.wait_ge(sL.h, sn[("b_spilled", c - 1)])
                    zg = fg - 4
                    sS.inc(sc.activation(sz_t[:, zg * 512:(zg + 1) * 512],
                                         pq[fg % 2][:], AF.Silu))
                    sn[("b_qd", c, fg)] = (sS.h, sS.n)

        @blk.tensor
        def _(t, c=c):
            if c >= 1:
                sem, val = sn[("b_badone", c - 1)]
                t.wait_ge(sem, val)
            for d in range(8):
                t.matmul(pl1b[:, 0:16], xnrfm(c % 2)[:, d * C:(d + 1) * C],
                         wba_bf[:, d * 16:(d + 1) * 16], start=(d == 0), stop=(d == 7))
            mi = None
            for d in range(8):
                mi = t.matmul(pl1b[:, 16:24], xnrfm(c % 2)[:, d * C:(d + 1) * C],
                              fgw_bf[:, d * 8:(d + 1) * 8], start=(d == 0), stop=(d == 7))
            sT.inc(mi)
        t_ba = sT.n

        @vec
        def _(v, c=c, t_ba=t_ba):
            v.wait_ge(sT.h, t_ba)
            v.tensor_copy(col_ba[:, c * 16:(c + 1) * 16], pl1b[:, 0:16])
            sV.inc(v.tensor_scalar_mul(col_fflog[:, c * 8:(c + 1) * 8],
                                       pl1b[:, 16:24], col_ir1[:, c:c + 1]))
            sn[("b_badone", c)] = (sV.h, sV.n)

        if c >= 1:
            for fgp in (0, 2):
                for fg in (fgp, fgp + 1):
                    @blk.tensor
                    def _(t, c=c, fg=fg):
                        sem, val = sn[("b_qd", c, 3)]
                        t.wait_ge(sem, val)
                        key = ("b_convd", c, fg - 2) if fg >= 2 else \
                            ("b_convd", c - 1, fg + 2)
                        if key in sn:
                            sem2, v2 = sn[key]
                            t.wait_ge(sem2, v2)
                        sl = slice(fg * 512, (fg + 1) * 512)
                        for j in range(KC):
                            t.matmul(pcv[fg % 2][:], shifts[:, j * C:(j + 1) * C],
                                     pre(c % 2, j)[:, sl], start=(j == 0), stop=False)
                        mi = None
                        for j in range(3):
                            mi = t.matmul(pcv[fg % 2][:], halos[:, j * C:(j + 1) * C],
                                          pre((c - 1) % 2, j)[:, sl],
                                          start=False, stop=(j == 2))
                        sT.inc(mi)
                        sn[("b_convmm", c, fg)] = sT.n
                        if fg == 3:
                            sn[("b_convdone", c)] = sT.n

                @blk.scalar
                def _(sc, c=c, fgp=fgp):
                    sc.wait_ge(sT.h, sn[("b_convmm", c, fgp + 1)])
                    if fgp == 0 and c >= 2:
                        sc.wait_ge(sL.h, sn[("b_spilled", c - 1)])
                    for fg in (fgp, fgp + 1):
                        sS.inc(sc.activation(qkvs_t[:, fg * 512:(fg + 1) * 512],
                                             pcv[fg % 2][:], AF.Silu))
                        sn[("b_convd", c, fg)] = (sS.h, sS.n)
                    if fgp == 2:
                        mi = None
                        for jj in range(8):
                            mi = sc.activation(
                                l2scr[:], qkvs_t[:, jj * C:(jj + 1) * C],
                                AF.Square,
                                accum_out=col_ssq[:, c * 8 + jj:c * 8 + jj + 1])
                        sS.inc(mi)
            s_l2 = sS.n

            @blk.sync
            def _(s, c=c, s_l2=s_l2):
                s.wait_ge(sS.h, s_l2)
                sL.inc(s.dma_start(out=sp_qkvs.ap()[c], in_=qkvs_t[:]), 16)
                sem, val = sn[("b_qd", c, 5)]
                s.wait_ge(sem, val)
                sL.inc(s.dma_start(out=sp_sz.ap()[c], in_=sz_t[:]), 16)
                sn[("b_spilled", c)] = sL.n
        else:
            @blk.sync
            def _(s, c=c):
                sem, val = sn[("b_qd", c, 5)]
                s.wait_ge(sem, val)
                sL.inc(s.dma_start(out=sp_sz.ap()[c], in_=sz_t[:]), 16)
                sn[("b_spilled", c)] = sL.n

    p1bs.close()
    emit_moe_jobs(len(moe_jobs) + 1)
    p01s.close()

    phase_gate(blk)
    p2s = ExitStack()
    wout_bf = p2s.enter_context(nc.sbuf_tensor("wout_bf", [C, 8 * HID], BF16))
    qkvs2_t = p2s.enter_context(nc.sbuf_tensor("qkvs2", [C, 2 * CONVD], BF16))
    qkvs2 = lambda i, _w=CONVD: qkvs2_t[:, i * _w:(i + 1) * _w]
    szf_t = p2s.enter_context(nc.sbuf_tensor("szf", [C, 2 * VAL], F32))
    szf = lambda i, _w=VAL: szf_t[:, i * _w:(i + 1) * _w]
    x2t_t = p2s.enter_context(nc.sbuf_tensor("x2t", [C, 2 * HID], F32))
    x2t = lambda i, _w=HID: x2t_t[:, i * _w:(i + 1) * _w]
    kqfm_t = p2s.enter_context(nc.sbuf_tensor("kqfm", [C, 2 * HID], BF16))
    kqfm = lambda i, _w=HID: kqfm_t[:, i * _w:(i + 1) * _w]
    ebuf_t = p2s.enter_context(nc.sbuf_tensor("ebuf", [C, 2 * 384], F32))
    ebuf = lambda i, _w=384: ebuf_t[:, i * _w:(i + 1) * _w]
    colsb = p2s.enter_context(nc.sbuf_tensor("colsb", [C, 64], F32))
    gtile = p2s.enter_context(nc.sbuf_tensor("gtile", [C, 8], F32))
    gcb = p2s.enter_context(nc.sbuf_tensor("gcb", [C, 8], F32))
    ecb = p2s.enter_context(nc.sbuf_tensor("ecb", [C, 8], F32))
    ecolK = p2s.enter_context(nc.sbuf_tensor("ecolK", [C, 8], F32))
    colA = p2s.enter_context(nc.sbuf_tensor("colA", [C, 8], F32))
    colSB = p2s.enter_context(nc.sbuf_tensor("colSB", [C, 8], F32))
    colQ = p2s.enter_context(nc.sbuf_tensor("colQ", [C, 8], F32))
    scr8 = p2s.enter_context(nc.sbuf_tensor("scr8", [C, 48], F32))
    dpre = p2s.enter_context(nc.sbuf_tensor("dpre", [C, 8 * 384], F32))
    nn_t = p2s.enter_context(nc.sbuf_tensor("nn", [C, 2 * 2 * (HV * DK)], BF16))
    nbh = lambda i, h: nn_t[:, i * 2048 + h * 256:i * 2048 + h * 256 + 128]
    ntbh = lambda i, h: nn_t[:, i * 2048 + h * 256 + 128:i * 2048 + (h + 1) * 256]
    nnh = lambda i, h: nn_t[:, i * 2048 + h * 256:i * 2048 + (h + 1) * 256]
    rbb_t = p2s.enter_context(nc.sbuf_tensor("rbb", [C, 2 * (HV * 256)], BF16))
    rbb = lambda i, _w=HV * 256: rbb_t[:, i * _w:(i + 1) * _w]
    wu2 = p2s.enter_context(nc.sbuf_tensor("wu2", [C, HV * 256], BF16))
    kpp = p2s.enter_context(nc.sbuf_tensor("kpp", [C, HV * DK], BF16))
    mtb = p2s.enter_context(nc.sbuf_tensor("mtb", [C, HV * DK], BF16))
    mub = p2s.enter_context(nc.sbuf_tensor("mub", [C, HV * DK], BF16))
    ztb = p2s.enter_context(nc.sbuf_tensor("ztb", [C, HV * DK], BF16))
    ptb = p2s.enter_context(nc.sbuf_tensor("ptb", [C, HV * DK], BF16))
    qtt = p2s.enter_context(nc.sbuf_tensor("qtt", [C, DK], BF16))
    qtfm = p2s.enter_context(nc.sbuf_tensor("qtfm", [C, HV * DK], BF16))
    otile = p2s.enter_context(nc.sbuf_tensor("otile", [C, VAL], F32))
    og = p2s.enter_context(nc.sbuf_tensor("og", [C, VAL], BF16))
    ogfm = p2s.enter_context(nc.sbuf_tensor("ogfm", [C, VAL], BF16))
    hsb = p2s.enter_context(nc.sbuf_tensor("hsb", [C, HID], F32))
    xnsb = p2s.enter_context(nc.sbuf_tensor("xnsb", [C, HID], F32))
    x2rt = p2s.enter_context(nc.sbuf_tensor("x2rt", [C, HID], BF16))
    x2rfm = p2s.enter_context(nc.sbuf_tensor("x2rfm", [C, HID], BF16))
    outp = p2s.enter_context(nc.sbuf_tensor("outp", [C, HID], F32))
    rs8 = p2s.enter_context(nc.sbuf_tensor("rs8", [C, 8], F32))
    smx = p2s.enter_context(nc.sbuf_tensor("smx", [C, 64], F32))
    rm8 = p2s.enter_context(nc.sbuf_tensor("rm8", [C, 8], F32))

    @vec
    def _(v):
        sV.inc(v.memset(rm8[:], 0.0))
    sqscr = p2s.enter_context(nc.sbuf_tensor("sqscr", [C, 512], F32))
    p_kk = p2s.enter_context(nc.psum_tensor("p_kk", [C, 512], F32))
    p_qk = p2s.enter_context(nc.psum_tensor("p_qk", [C, 512], F32))
    p_r0 = p2s.enter_context(nc.psum_tensor("p_r0", [C, 384], F32))
    p_r1 = p2s.enter_context(nc.psum_tensor("p_r1", [C, 384], F32))
    p_s0 = p2s.enter_context(nc.psum_tensor("p_s0", [C, 384], F32))
    p_s1 = p2s.enter_context(nc.psum_tensor("p_s1", [C, 384], F32))
    p_tb = p2s.enter_context(nc.psum_tensor("p_tb", [C, HID], BF16))
    p_sm = p2s.enter_context(nc.psum_tensor("p_sm", [C, 256], F32))
    p_r = [p_r0, p_r1]
    p_sx = [p_s0, p_s1]

    def khd(h):
        return slice(h * DK, (h + 1) * DK)

    @blk.gpsimd
    def _(g):
        for d in range(8):
            sG.inc(g.dma_start(out=wout_bf[:, d * HID:(d + 1) * HID],
                               in_=W_out.ap()[d * C:(d + 1) * C, :]), 16)
    gl_wout = sG.n

    for c in range(1, NCH):
        own = c >= NPRE
        co = c - NPRE

        @blk.sync
        def _(s, c=c, own=own):
            if ("p2_done", c - 2) in sn:
                sem, val = sn[("p2_done", c - 2)]
                s.wait_ge(sem, val)
            s.wait_ge(sL.h, sn[("b_spilled", c)])
            sL.inc(s.dma_start(out=qkvs2(c % 2), in_=sp_qkvs.ap()[c]), 16)
            if own:
                sL.inc(s.dma_start(out=x2t(c % 2),
                                   in_=x_h.ap()[c * C:(c + 1) * C, :]), 16)
        p2_ld = sL.n

        if own:
            @blk.gpsimd
            def _(g, c=c):
                if ("p2_done", c - 2) in sn:
                    sem, val = sn[("p2_done", c - 2)]
                    g.wait_ge(sem, val)
                sG.inc(g.dma_start(out=szf(c % 2), in_=sp_sz.ap()[c]), 16)
        p2_gld = sG.n


        @vec
        def _(v, c=c):
            if ("p2_cols_free", c - 1) in sn:
                sem, val = sn[("p2_cols_free", c - 1)]
                v.wait_ge(sem, val)
            if ("p2_gcb_t", c - 1) in sn:
                v.wait_ge(sT.h, sn[("p2_gcb_t", c - 1)])
            if ("p2_dump", c - 1) in sn:
                v.wait_ge(sL.h, sn[("p2_dump", c - 1)])
            v.tensor_tensor(scr8[:, 0:8], col_ba[:, c * 16 + 8:c * 16 + 16],
                            dt_b[:], ALU.add)
            sV.inc(v.tensor_scalar_mul(scr8[:, 32:40],
                                       col_ba[:, c * 16:c * 16 + 8],
                                       cst[:, 9:10]))

        @blk.scalar
        def _(sc, c=c):
            sc.wait_ge(sV.h, sV.n)
            sc.activation(scr8[:, 8:16], scr8[:, 0:8], AF.Abs)
            sc.activation(scr8[:, 8:16], scr8[:, 8:16], AF.Exp, scale=-1.0)
            sc.activation(scr8[:, 8:16], scr8[:, 8:16], AF.Ln, bias=1.0)
            sc.activation(scr8[:, 40:48], scr8[:, 32:40], AF.Abs)
            sc.activation(scr8[:, 40:48], scr8[:, 40:48], AF.Exp, scale=-1.0)
            sc.activation(scr8[:, 40:48], scr8[:, 40:48], AF.Ln, bias=1.0)
            sc.activation(scr8[:, 24:28], col_ssq[:, c * 8 + 4:c * 8 + 8],
                          AF.Ln, bias=EPS)
            sS.inc(sc.activation(scr8[:, 28:32], col_ssq[:, c * 8:c * 8 + 4],
                                 AF.Ln, bias=EPS))

        @vec
        def _(v, c=c):
            v.wait_ge(sS.h, sS.n)
            v.tensor_scalar_max(scr8[:, 16:24], scr8[:, 0:8], cst[:, 10:11])
            v.tensor_tensor(scr8[:, 16:24], scr8[:, 16:24], scr8[:, 8:16],
                            ALU.add)
            sV.inc(v.scalar_tensor_tensor(gtile[:], scr8[:, 16:24],
                                          cst[:, 9:10], expA_b[:],
                                          ALU.mult, ALU.mult))
        v_gt = sV.n

        @blk.tensor
        def _(t, c=c, v_gt=v_gt):
            t.wait_ge(sV.h, v_gt)
            if ("p2_psm_free", c - 1) in sn:
                sem, val = sn[("p2_psm_free", c - 1)]
                t.wait_ge(sem, val)
            t.matmul(p_sm[:, 0:8], utf[:], gtile[:], start=True, stop=True)
            sT.inc(t.matmul(p_sm[:, 8:16], r127f[:], gtile[:],
                            start=True, stop=True))
        t_g = sT.n
        sn[("p2_gcb_t", c)] = t_g

        @vec
        def _(v, c=c, own=own, t_g=t_g):
            v.wait_ge(sT.h, t_g)
            v.tensor_scalar_max(scr8[:, 16:24], scr8[:, 32:40], cst[:, 10:11])
            v.tensor_tensor(scr8[:, 16:24], scr8[:, 16:24], scr8[:, 40:48],
                            ALU.add)
            for h in range(8):
                v.tensor_scalar_mul(colsb[:, 48 + h:49 + h],
                                    scr8[:, 24 + h // 2:25 + h // 2],
                                    cst[:, 5:6])
            v.tensor_scalar_mul(colsb[:, 40:48], scr8[:, 16:24], cst[:, 5:6])
            if own:
                for h in range(8):
                    v.tensor_scalar(colsb[:, 56 + h:57 + h],
                                    scr8[:, 28 + h // 2:29 + h // 2],
                                    cst[:, 5:6], cst[:, 7:8],
                                    ALU.mult, ALU.add)
            v.tensor_copy(gcb[:], p_sm[:, 8:16])
            v.tensor_copy(colsb[:, 32:40], p_sm[:, 0:8])
            v.tensor_tensor(scr8[:, 8:16], colsb[:, 40:48], colsb[:, 48:56],
                            ALU.add)
            v.tensor_tensor(colsb[:, 0:8], p_sm[:, 0:8], scr8[:, 8:16],
                            ALU.add)
            v.tensor_tensor(colsb[:, 8:16], scr8[:, 8:16], p_sm[:, 0:8],
                            ALU.subtract)
            if own:
                v.tensor_tensor(colsb[:, 16:24], p_sm[:, 0:8],
                                colsb[:, 56:64], ALU.add)
            v.tensor_tensor(colsb[:, 24:32], colsb[:, 48:56], p_sm[:, 0:8],
                            ALU.subtract)
            sV.inc(v.tensor_tensor(scr8[:, 8:16], colsb[:, 24:32], gcb[:],
                                   ALU.add))
        v_cab = sV.n
        v_cols1 = v_cab

        @blk.scalar
        def _(sc, c=c, own=own, v_cab=v_cab):
            sc.wait_ge(sV.h, v_cab)
            sc.activation(ecb[:], gcb[:], AF.Exp)
            sc.activation(ecolK[:], scr8[:, 8:16], AF.Exp)
            sc.activation(colA[:], colsb[:, 0:8], AF.Exp)
            mi = sc.activation(colSB[:], colsb[:, 40:48], AF.Exp)
            if own:
                mi = sc.activation(colQ[:], colsb[:, 16:24], AF.Exp)
            sS.inc(mi)
        s_cols = sS.n

        sn[("p2_psm_free", c)] = (sV.h, v_cab)
        v_rows = v_cab

        @blk.tensor
        def _(t, c=c, own=own, p2_ld=p2_ld):
            t.wait_ge(sL.h, p2_ld)
            if ("p2_ptb_free", c - 1) in sn:
                sem, val = sn[("p2_ptb_free", c - 1)]
                t.wait_ge(sem, val)
            mi = None
            for j in range(4):
                mi = t.transpose(p_tb[:, j * C:(j + 1) * C],
                                 qkvs2(c % 2)[:, KEY + j * C:KEY + (j + 1) * C],
                                 ibf[:])
            if own:
                for j in range(4):
                    mi = t.transpose(p_tb[:, 512 + j * C:512 + (j + 1) * C],
                                     qkvs2(c % 2)[:, j * C:(j + 1) * C], ibf[:])
            sT.inc(mi)
        t_kqt = sT.n

        @vec
        def _(v, c=c, own=own, t_kqt=t_kqt):
            v.wait_ge(sT.h, t_kqt)
            if own:
                sV.inc(v.tensor_copy(kqfm(c % 2), p_tb[:]))
            else:
                sV.inc(v.tensor_copy(kqfm(c % 2)[:, 0:512], p_tb[:, 0:512]))
            sn[("p2_kqfm", c)] = (sV.h, sV.n)
        s_kq = (sV.h, sV.n)

        @blk.tensor
        def _(t, c=c, own=own, s_kq=s_kq):
            t.wait_ge(s_kq[0], s_kq[1])
            if ("p2_kkfree", c - 1) in sn:
                sem, val = sn[("p2_kkfree", c - 1)]
                t.wait_ge(sem, val)
            mi = None
            for j in range(4):
                mi = t.matmul(p_kk[:, j * C:(j + 1) * C],
                              kqfm(c % 2)[:, j * C:(j + 1) * C],
                              kqfm(c % 2)[:, j * C:(j + 1) * C],
                              start=True, stop=True)
            sT.inc(mi)
            if own:
                for j in range(4):
                    mi = t.matmul(p_qk[:, j * C:(j + 1) * C],
                                  kqfm(c % 2)[:, j * C:(j + 1) * C],
                                  kqfm(c % 2)[:, 512 + j * C:512 + (j + 1) * C],
                                  start=True, stop=True)
                sT.inc(mi)
        t_kk = sT.n

        @vec
        def _(v, c=c, own=own):
            if ("p2_dmm", c - 1, 7) in sn:
                v.wait_ge(sT.h, sn[("p2_dmm", c - 1, 7)])
            mi = None
            for h in range(8):
                v.tensor_scalar_add(dpre[:, h * 384:h * 384 + 128],
                                    masks[:, 0:128], colsb[:, h:h + 1])
                mi = v.tensor_scalar_add(dpre[:, h * 384 + 128:h * 384 + 256],
                                         masks[:, 128:256],
                                         colsb[:, 8 + h:9 + h])
                if own:
                    mi = v.tensor_scalar_add(dpre[:, h * 384 + 256:h * 384 + 384],
                                             masks[:, 256:384],
                                             colsb[:, 16 + h:17 + h])
            sV.inc(mi)
        v_dpre = sV.n

        for hp in (0, 2, 4, 6):
            for h in (hp, hp + 1):
                j = h // 2

                @blk.tensor
                def _(t, c=c, h=h, own=own, v_dpre=v_dpre):
                    if h == 0:
                        t.wait_ge(sV.h, v_dpre)
                        if ("p2_post_free", c - 1) in sn:
                            sem, val = sn[("p2_post_free", c - 1)]
                            t.wait_ge(sem, val)
                    if h >= 2:
                        sem, val = sn[("p2_eexp", c, h - 2)]
                        t.wait_ge(sem, val)
                    t.transpose(p_r[h % 2][:, 0:128],
                                dpre[:, h * 384:h * 384 + 128], if32[:])
                    mi = t.transpose(p_r[h % 2][:, 128:256],
                                     dpre[:, h * 384 + 128:h * 384 + 256],
                                     if32[:])
                    if own:
                        mi = t.transpose(p_r[h % 2][:, 256:384],
                                         dpre[:, h * 384 + 256:h * 384 + 384],
                                         if32[:])
                    sT.inc(mi)
                    sn[("p2_dmm", c, h)] = sT.n

            @blk.scalar
            def _(sc, c=c, hp=hp, own=own, s_cols=s_cols):
                sc.wait_ge(sT.h, sn[("p2_dmm", c, hp + 1)])
                if hp == 0:
                    sc.wait_ge(sS.h, s_cols)
                    if ("p2_ntn", c - 1, 7) in sn:
                        sem, val = sn[("p2_ntn", c - 1, 7)]
                        sc.wait_ge(sem, val)
                else:
                    sem, val = sn[("p2_ntn", c, hp - 1)]
                    sc.wait_ge(sem, val)
                for h in (hp, hp + 1):
                    sc.activation(ebuf(h % 2)[:, 0:128], p_r[h % 2][:, 0:128],
                                  AF.Exp, bias=colsb[:, 8 + h:9 + h])
                    mi = sc.activation(ebuf(h % 2)[:, 128:256],
                                       p_r[h % 2][:, 128:256],
                                       AF.Exp, bias=colsb[:, h:h + 1])
                    if own:
                        mi = sc.activation(ebuf(h % 2)[:, 256:384],
                                           p_r[h % 2][:, 256:384],
                                           AF.Exp, bias=colsb[:, 24 + h:25 + h])
                    sS.inc(mi)
                    sn[("p2_eexp", c, h)] = (sS.h, sS.n)

            for h in (hp, hp + 1):
                j = h // 2

                @vec
                def _(v, c=c, h=h, j=j, own=own, t_kk=t_kk):
                    sem, val = sn[("p2_eexp", c, h)]
                    v.wait_ge(sem, val)
                    if h == 0:
                        v.wait_ge(sT.h, t_kk)
                    if ("p2_scan_mm", c - 1, h) in sn:
                        v.wait_ge(sT.h, sn[("p2_scan_mm", c - 1, h)])
                    v.scalar_tensor_tensor(ntbh(0, h),
                                           p_kk[:, j * C:(j + 1) * C],
                                           cst[:, 9:10],
                                           ebuf(h % 2)[:, 0:128],
                                           ALU.mult, ALU.mult)
                    mi = v.scalar_tensor_tensor(nbh(0, h),
                                                p_kk[:, j * C:(j + 1) * C],
                                                cst[:, 9:10],
                                                ebuf(h % 2)[:, 128:256],
                                                ALU.mult, ALU.mult)
                    if own:
                        mi = v.tensor_tensor(mtb[:, khd(h)],
                                             p_qk[:, j * C:(j + 1) * C],
                                             ebuf(h % 2)[:, 256:384], ALU.mult)
                    v.tensor_scalar_mul(rbb(0)[:, h * 256:h * 256 + 128],
                                        qkvs2(c % 2)[:, KEY + j * C:KEY + (j + 1) * C],
                                        colA[:, h:h + 1])
                    mi = v.tensor_scalar_mul(
                        rbb(0)[:, h * 256 + 128:(h + 1) * 256],
                        qkvs2(c % 2)[:, 2 * KEY + h * C:2 * KEY + (h + 1) * C],
                        colSB[:, h:h + 1])
                    sV.inc(mi)
                    sn[("p2_ntn", c, h)] = (sV.h, sV.n)

        for li, lv in enumerate(LVLS):
            src = li % 2
            dst = (li + 1) % 2
            for h in range(8):
                @blk.tensor
                def _(t, c=c, li=li, h=h, src=src):
                    if h < 2:
                        if li == 0:
                            sem, val = sn[("p2_ntn", c, h)]
                            t.wait_ge(sem, val)
                            sem, val = sn[("p2_eexp", c, 6 + h)]
                            t.wait_ge(sem, val)
                        elif li - 1 <= LSQ:
                            sem, val = sn[("p2_sq_ev", c, li - 1, h)]
                            t.wait_ge(sem, val)
                        else:
                            sem, val = sn[("p2_r_ev", c, li - 1, 6 + h)]
                            t.wait_ge(sem, val)
                    if h >= 2:
                        sem, val = sn[("p2_r_ev", c, li, h - 2)]
                        t.wait_ge(sem, val)
                    t.matmul(p_r[h % 2][:, 0:256], ntbh(src, h),
                             rbb(src)[:, h * 256:(h + 1) * 256],
                             start=True, stop=False)
                    sT.inc(t.matmul(p_r[h % 2][:, 0:256], ibf[:],
                                    rbb(src)[:, h * 256:(h + 1) * 256],
                                    start=False, stop=True))
                    sn[("p2_r_mm", c, li, h)] = sT.n

                @vec
                def _(v, c=c, li=li, h=h, dst=dst):
                    v.wait_ge(sT.h, sn[("p2_r_mm", c, li, h)])
                    sV.inc(v.tensor_copy(rbb(dst)[:, h * 256:(h + 1) * 256],
                                         p_r[h % 2][:, 0:256]))
                    sn[("p2_r_ev", c, li, h)] = (sV.h, sV.n)

            if li <= LSQ:
                last = li == LSQ
                for h in range(8):
                    @blk.tensor
                    def _(t, c=c, li=li, h=h, src=src, last=last):
                        if h < 2:
                            if li == 0:
                                if ("p2_scan_ev", c - 1, 7) in sn:
                                    sem, val = sn[("p2_scan_ev", c - 1, 7)]
                                    t.wait_ge(sem, val)
                            else:
                                sem, val = sn[("p2_sq_ev", c, li - 1, 6 + h)]
                                t.wait_ge(sem, val)
                        else:
                            sem, val = sn[("p2_sq_ev", c, li, h - 2)]
                            t.wait_ge(sem, val)
                        if not last:
                            t.matmul(p_sx[h % 2][:, 0:128], ntbh(src, h),
                                     nbh(src, h), start=True, stop=True)
                        sT.inc(t.matmul(p_sx[h % 2][:, 128:256],
                                        nbh(src, h),
                                        ntbh(src, h), start=True, stop=True))
                        sn[("p2_sq_mm", c, li, h)] = sT.n

                    @vec
                    def _(v, c=c, li=li, h=h, dst=dst, last=last):
                        v.wait_ge(sT.h, sn[("p2_sq_mm", c, li, h)])
                        if not last:
                            mi = v.tensor_copy(nnh(dst, h), p_sx[h % 2][:, 0:256])
                        else:
                            mi = v.tensor_copy(ntbh(dst, h),
                                               p_sx[h % 2][:, 128:256])
                        sV.inc(mi)
                        sn[("p2_sq_ev", c, li, h)] = (sV.h, sV.n)

        rfin = len(LVLS) % 2

        for h in range(8):
            j = h // 2

            @vec
            def _(v, c=c, h=h, j=j, rfin=rfin):
                sem, val = sn[("p2_r_ev", c, len(LVLS) - 1, h)]
                v.wait_ge(sem, val)
                if ("p2_scan_mm", c - 1, h) in sn:
                    v.wait_ge(sT.h, sn[("p2_scan_mm", c - 1, h)])
                v.tensor_scalar_mul(wu2[:, h * 256:h * 256 + 128],
                                    rbb(rfin)[:, h * 256:h * 256 + 128],
                                    colSB[:, h:h + 1])
                v.tensor_scalar_mul(wu2[:, h * 256 + 128:(h + 1) * 256],
                                    rbb(rfin)[:, h * 256 + 128:(h + 1) * 256],
                                    colSB[:, h:h + 1])
                sV.inc(v.tensor_scalar_mul(
                    kpp[:, khd(h)],
                    qkvs2(c % 2)[:, KEY + j * C:KEY + (j + 1) * C],
                    ecolK[:, h:h + 1]))
                sn[("p2_wu", c, h)] = sV.n

            if own:
                @blk.tensor
                def _(t, c=c, h=h):
                    t.wait_ge(sV.h, sn[("p2_wu", c, h)])
                    if h < 2:
                        sem, val = sn[("p2_sq_ev", c, LSQ, 6 + h)]
                        t.wait_ge(sem, val)
                    else:
                        sem, val = sn[("p2_scan_ev", c, h - 2)]
                        t.wait_ge(sem, val)
                    sT.inc(t.matmul(p_sx[h % 2][:, 0:128],
                                    wu2[:, h * 256:h * 256 + 128],
                                    mtb[:, khd(h)], start=True, stop=True))
                    sn[("p2_mumw", c, h)] = sT.n

                @vec
                def _(v, c=c, h=h):
                    v.wait_ge(sT.h, sn[("p2_mumw", c, h)])
                    sV.inc(v.tensor_scalar_mul(qtfm[:, khd(h)],
                                               p_sx[h % 2][:, 0:128],
                                               cst[:, 9:10]))
                    sn[("p2_muq", c, h)] = (sV.h, sV.n)

            @blk.tensor
            def _(t, c=c, h=h, own=own):
                if not own:
                    t.wait_ge(sV.h, sn[("p2_wu", c, h)])
                if h < 2:
                    sem, val = sn[("p2_r_ev", c, len(LVLS) - 1, 6 + h)]
                    t.wait_ge(sem, val)
                else:
                    sem, val = sn[("p2_ptz", c, h - 2)]
                    t.wait_ge(sem, val)
                t.matmul(p_r[h % 2][:, 0:128], wu2[:, h * 256:h * 256 + 128],
                         kpp[:, khd(h)], start=True, stop=True)
                sT.inc(t.matmul(p_r[h % 2][:, 128:256], kpp[:, khd(h)],
                                wu2[:, h * 256 + 128:(h + 1) * 256],
                                start=True, stop=True))
                sn[("p2_ptzmm", c, h)] = sT.n

            @vec
            def _(v, c=c, h=h):
                v.wait_ge(sT.h, sn[("p2_ptzmm", c, h)])
                v.scalar_tensor_tensor(ptb[:, khd(h)], ibf[:], ecb[:, h:h + 1],
                                       p_r[h % 2][:, 0:128], ALU.mult, ALU.subtract)
                sV.inc(v.tensor_copy(ztb[:, khd(h)], p_r[h % 2][:, 128:256]))
                sn[("p2_ptz", c, h)] = (sV.h, sV.n)

            scur = s_state[:, (c % 2) * VAL + h * DK:(c % 2) * VAL + (h + 1) * DK]
            snew = s_state[:, ((c + 1) % 2) * VAL + h * DK:((c + 1) % 2) * VAL + (h + 1) * DK]

            @blk.tensor
            def _(t, c=c, h=h, j=j, own=own, scur=scur):
                sem, val = sn[("p2_ptz", c, h)]
                t.wait_ge(sem, val)
                if own:
                    sem2, v2 = sn[("p2_muq", c, h)]
                    t.wait_ge(sem2, v2)
                if h >= 2:
                    sem3, v3 = sn[("p2_scan_ev", c, h - 2)]
                    t.wait_ge(sem3, v3)
                if own:
                    t.matmul(p_sx[h % 2][:, 128:256], mtb[:, khd(h)],
                             wu2[:, h * 256 + 128:(h + 1) * 256],
                             start=True, stop=False)
                    t.matmul(p_sx[h % 2][:, 128:256], qtfm[:, khd(h)], scur,
                             start=False, stop=True)
                    t.matmul(p_sx[h % 2][:, 0:128],
                             kqfm(c % 2)[:, 512 + j * C:512 + (j + 1) * C],
                             scur, start=True, stop=True)
                t.matmul(p_sx[h % 2][:, 256:384], ptb[:, khd(h)], scur,
                         start=True, stop=False)
                sT.inc(t.matmul(p_sx[h % 2][:, 256:384], ibf[:], ztb[:, khd(h)],
                                start=False, stop=True))
                sn[("p2_scan_mm", c, h)] = sT.n

            @vec
            def _(v, c=c, h=h, own=own, snew=snew):
                v.wait_ge(sT.h, sn[("p2_scan_mm", c, h)])
                if own:
                    v.tensor_scalar_mul(otile[:, khd(h)], p_sx[h % 2][:, 0:128],
                                        colQ[:, h:h + 1])
                    v.tensor_tensor(otile[:, khd(h)], otile[:, khd(h)],
                                    p_sx[h % 2][:, 128:256], ALU.add)
                sV.inc(v.tensor_copy(snew, p_sx[h % 2][:, 256:384]))
                sn[("p2_scan_ev", c, h)] = (sV.h, sV.n)
        sn[("p2_post_free", c)] = sn[("p2_scan_ev", c, 7)]

        if not own:
            sn[("p2_done", c)] = sn[("p2_scan_ev", c, 7)]
            sn[("p2_cols_free", c)] = sn[("p2_eexp", c, 7)]
            sn[("p2_kkfree", c)] = sn[("p2_ntn", c, 7)]
            sn[("p2_ptb_free", c)] = sn[("p2_kqfm", c)]
            continue

        if dbg and c == NPRE:
            @blk.sync
            def _(s, c=c):
                sem, val = sn[("p2_scan_ev", c, 7)]
                s.wait_ge(sem, val)
                sL.inc(s.dma_start(out=sp_dbgf.ap()[:, 0:64], in_=colsb[:]), 16)
                sL.inc(s.dma_start(out=sp_dbgf.ap()[:, 64:72], in_=gtile[:]), 16)
                sL.inc(s.dma_start(out=sp_dbgf.ap()[:, 72:80], in_=colA[:]), 16)
                sL.inc(s.dma_start(out=sp_dbgf.ap()[:, 80:88], in_=colSB[:]), 16)
                sL.inc(s.dma_start(out=sp_dbgf.ap()[:, 88:96], in_=colQ[:]), 16)
                sL.inc(s.dma_start(out=sp_dbgf.ap()[:, 96:104], in_=ecolK[:]), 16)
                sL.inc(s.dma_start(out=sp_dbgf.ap()[:, 104:112], in_=ecb[:]), 16)
                sL.inc(s.dma_start(out=sp_dbgf.ap()[:, 112:120], in_=gcb[:]), 16)
                sL.inc(s.dma_start(out=sp_dbgf.ap()[:, 120:128], in_=expA_b[:]), 16)
                sL.inc(s.dma_start(out=sp_dbgf.ap()[:, 128:136], in_=dt_b[:]), 16)
                sL.inc(s.dma_start(out=sp_dbgf.ap()[:, 136:152],
                                   in_=col_ba[:, c * 16:(c + 1) * 16]), 16)
                sL.inc(s.dma_start(out=sp_dbgf.ap()[:, 152:160],
                                   in_=col_ssq[:, c * 8:(c + 1) * 8]), 16)
                sL.inc(s.dma_start(out=sp_dbgb.ap()[:, 0:256], in_=wu2[:, 0:256]), 16)
                sL.inc(s.dma_start(out=sp_dbgb.ap()[:, 256:384], in_=kpp[:, 0:128]), 16)
                sL.inc(s.dma_start(out=sp_dbgb.ap()[:, 384:512], in_=mtb[:, 0:128]), 16)
                sL.inc(s.dma_start(out=sp_dbgb.ap()[:, 512:640], in_=mub[:, 0:128]), 16)
                sL.inc(s.dma_start(out=sp_dbgb.ap()[:, 640:768], in_=ztb[:, 0:128]), 16)
                sL.inc(s.dma_start(out=sp_dbgb.ap()[:, 768:896], in_=ptb[:, 0:128]), 16)
                sL.inc(s.dma_start(out=sp_dbgb.ap()[:, 896:1024], in_=qtfm[:, 0:128]), 16)
                sL.inc(s.dma_start(out=sp_dbgb.ap()[:, 1024:1280], in_=rbb(1)[:, 0:256]), 16)
                sL.inc(s.dma_start(out=sp_dbgb.ap()[:, 1280:1408], in_=ntbh(0, 0)), 16)
                sL.inc(s.dma_start(out=sp_dbgb.ap()[:, 1408:1536],
                                   in_=s_state[:, ((c + 1) % 2) * VAL:((c + 1) % 2) * VAL + 128]), 16)
                sn[("p2_dump", c)] = sL.n

        @blk.scalar
        def _(sc, c=c):
            sem, val = sn[("p2_scan_ev", c, 7)]
            sc.wait_ge(sem, val)
            for h in range(8):
                sc.activation(sqscr[:, 0:128], otile[:, khd(h)], AF.Square,
                              accum_out=rs8[:, h:h + 1])
            sc.copy(rm8[:, 4:5], rm8[:, 4:5])
            sc.activation(rs8[:], rs8[:], AF.Ln, bias=EPS, scale=1.0 / DV)
            sc.copy(rm8[:, 4:5], rm8[:, 4:5])
            sS.inc(sc.activation(rs8[:], rs8[:], AF.Exp, scale=-0.5))
        s_rs = sS.n

        @vec
        def _(v, c=c, s_rs=s_rs, p2_gld=p2_gld):
            v.wait_ge(sS.h, s_rs)
            v.wait_ge(sG.h, p2_gld)
            mi = None
            for h in range(8):
                mi = v.scalar_tensor_tensor(og[:, khd(h)], otile[:, khd(h)],
                                            rs8[:, h:h + 1],
                                            szf(c % 2)[:, khd(h)],
                                            ALU.mult, ALU.mult)
            sV.inc(mi)
            sn[("p2_og", c)] = sV.n
        v_og = sV.n

        @blk.tensor
        def _(t, c=c, v_og=v_og):
            t.wait_ge(sV.h, v_og)
            mi = None
            for k in range(8):
                mi = t.transpose(p_tb[:, k * C:(k + 1) * C], og[:, k * C:(k + 1) * C],
                                 ibf[:])
            sT.inc(mi)
        t_ogt = sT.n

        @vec
        def _(v, c=c, t_ogt=t_ogt):
            v.wait_ge(sT.h, t_ogt)
            sV.inc(v.tensor_scalar_mul(ogfm[:], p_tb[:], gnw_c[:, 0:1]))
        v_ogfm = sV.n

        @blk.tensor
        def _(t, c=c, v_ogfm=v_ogfm):
            t.wait_ge(sV.h, v_ogfm)
            if c == NPRE:
                t.wait_ge(sG.h, gl_wout)
            for dh in range(2):
                ps = p_kk if dh == 0 else p_qk
                mi = None
                for d in range(8):
                    mi = t.matmul(ps[:], ogfm[:, d * C:(d + 1) * C],
                                  wout_bf[:, d * HID + dh * 512:d * HID + (dh + 1) * 512],
                                  start=(d == 0), stop=(d == 7))
                sT.inc(mi)
        t_wo = sT.n

        @vec
        def _(v, c=c, t_wo=t_wo):
            v.wait_ge(sT.h, t_wo)
            v.tensor_scalar(xnsb[:], x2t(c % 2), col_m[:, c:c + 1],
                            col_rstd[:, c:c + 1], ALU.subtract, ALU.mult)
            v.tensor_tensor(hsb[:, 0:512], p_kk[:], xnsb[:, 0:512], ALU.add)
            sV.inc(v.tensor_tensor(hsb[:, 512:1024], p_qk[:], xnsb[:, 512:1024], ALU.add))
        v_h = sV.n
        sn[("p2_kkfree", c)] = (sV.h, v_h)

        @blk.scalar
        def _(sc, c=c, v_h=v_h):
            sc.wait_ge(sV.h, v_h)
            sc.activation(sqscr[:], hsb[:, 0:512], AF.Square, accum_out=rm8[:, 0:1])
            sS.inc(sc.activation(sqscr[:], hsb[:, 512:1024], AF.Square,
                                 accum_out=rm8[:, 1:2]))
        s_h2 = sS.n

        @vec
        def _(v, c=c, s_h2=s_h2):
            v.wait_ge(sS.h, s_h2)
            sV.inc(v.tensor_tensor(rm8[:, 0:1], rm8[:, 0:1], rm8[:, 1:2], ALU.add))

        @blk.scalar
        def _(sc, c=c):
            sc.wait_ge(sV.h, sV.n)
            sc.activation(rm8[:, 1:2], rm8[:, 0:1], AF.Ln, bias=EPS, scale=1.0 / HID)
            sc.copy(rm8[:, 2:3], rm8[:, 2:3])
            sS.inc(sc.activation(rm8[:, 1:2], rm8[:, 1:2], AF.Exp, scale=-0.5))
        s_r2 = sS.n

        @vec
        def _(v, c=c, s_r2=s_r2):
            v.wait_ge(sS.h, s_r2)
            sV.inc(v.tensor_scalar_mul(x2rt[:], hsb[:], rm8[:, 1:2]))
        v_x2r = sV.n

        @blk.tensor
        def _(t, c=c, v_x2r=v_x2r):
            t.wait_ge(sV.h, v_x2r)
            mi = None
            for k in range(8):
                mi = t.transpose(p_tb[:, k * C:(k + 1) * C], x2rt[:, k * C:(k + 1) * C],
                                 ibf[:])
            sT.inc(mi)
        t_x2t = sT.n

        @vec
        def _(v, c=c, t_x2t=t_x2t):
            v.wait_ge(sT.h, t_x2t)
            if ("p2_osp", c - 1) in sn:
                v.wait_ge(sL.h, sn[("p2_osp", c - 1)])
            mi = None
            for k in range(8):
                mi = v.tensor_scalar_mul(x2rfm[:, k * C:(k + 1) * C],
                                         p_tb[:, k * C:(k + 1) * C],
                                         ln12[:, 8 + k:9 + k])
            sV.inc(mi)
            sn[("p2_ptb_free", c)] = (sV.h, sV.n)
        v_x2fm = sV.n

        @blk.tensor
        def _(t, c=c, v_x2fm=v_x2fm):
            t.wait_ge(sV.h, v_x2fm)
            mi = None
            for d in range(8):
                mi = t.matmul(p_sm[:, 16:24], x2rfm[:, d * C:(d + 1) * C],
                              wrt_bf[:, d * 8:(d + 1) * 8], start=(d == 0), stop=(d == 7))
            sT.inc(mi)
        t_rt = sT.n

        @vec
        def _(v, c=c, t_rt=t_rt):
            v.wait_ge(sT.h, t_rt)
            v.reduce_max(smx[:, 0:1], p_sm[:, 16:24], axis=AX.X)
            v.reduce_max(smx[:, 24:25], col_fflog[:, c * 8:(c + 1) * 8], axis=AX.X)
            v.tensor_scalar_sub(smx[:, 8:16], p_sm[:, 16:24], smx[:, 0:1])
            sV.inc(v.tensor_scalar_sub(smx[:, 32:40], col_fflog[:, c * 8:(c + 1) * 8],
                                       smx[:, 24:25]))

        @blk.scalar
        def _(sc, c=c):
            sc.wait_ge(sV.h, sV.n)
            sc.activation(smx[:, 8:16], smx[:, 8:16], AF.Exp)
            sS.inc(sc.activation(smx[:, 32:40], smx[:, 32:40], AF.Exp))

        @vec
        def _(v, c=c, co=co):
            v.wait_ge(sS.h, sS.n)
            v.reduce_sum(smx[:, 1:2], smx[:, 8:16], axis=AX.X)
            v.reduce_sum(smx[:, 25:26], smx[:, 32:40], axis=AX.X)
            v.reciprocal(smx[:, 2:3], smx[:, 1:2])
            v.reciprocal(smx[:, 26:27], smx[:, 25:26])
            v.tensor_scalar_mul(smx[:, 8:16], smx[:, 8:16], smx[:, 2:3])
            v.tensor_scalar_mul(smx[:, 32:40], smx[:, 32:40], smx[:, 26:27])
            v.reduce_max(smx[:, 0:1], smx[:, 8:16], axis=AX.X)
            v.reduce_max(smx[:, 24:25], smx[:, 32:40], axis=AX.X)
            v.tensor_scalar(smx[:, 16:24], smx[:, 8:16], smx[:, 0:1], None, ALU.is_equal)
            v.tensor_scalar(smx[:, 40:48], smx[:, 32:40], smx[:, 24:25], None,
                            ALU.is_equal)
            v.scalar_tensor_tensor(smx[:, 48:56], smx[:, 16:24], cst[:, 8:9],
                                   smx[:, 8:16], ALU.mult, ALU.add)
            v.scalar_tensor_tensor(smx[:, 56:64], smx[:, 40:48], cst[:, 8:9],
                                   smx[:, 32:40], ALU.mult, ALU.add)
            v.reduce_max(smx[:, 3:4], smx[:, 48:56], axis=AX.X)
            v.reduce_max(smx[:, 27:28], smx[:, 56:64], axis=AX.X)
            v.tensor_tensor(smx[:, 4:5], smx[:, 0:1], smx[:, 3:4], ALU.add)
            v.tensor_tensor(smx[:, 28:29], smx[:, 24:25], col_ir1[:, c:c + 1], ALU.mult)
            v.reciprocal(smx[:, 5:6], smx[:, 4:5])
            v.tensor_tensor(smx[:, 29:30], smx[:, 27:28], col_ir1[:, c:c + 1], ALU.mult)
            v.tensor_tensor(smx[:, 6:7], smx[:, 0:1], smx[:, 5:6], ALU.mult)
            v.tensor_tensor(smx[:, 7:8], smx[:, 3:4], smx[:, 5:6], ALU.mult)
            v.tensor_scalar(smx[:, 16:24], smx[:, 8:16], smx[:, 0:1], smx[:, 6:7],
                            ALU.is_equal, ALU.mult)
            v.tensor_scalar(smx[:, 48:56], smx[:, 8:16], smx[:, 3:4], smx[:, 7:8],
                            ALU.is_equal, ALU.mult)
            v.tensor_scalar(smx[:, 40:48], smx[:, 32:40], smx[:, 24:25], smx[:, 28:29],
                            ALU.is_equal, ALU.mult)
            v.tensor_scalar(smx[:, 56:64], smx[:, 32:40], smx[:, 27:28], smx[:, 29:30],
                            ALU.is_equal, ALU.mult)
            v.tensor_tensor(col_route[:, co * 8:(co + 1) * 8], smx[:, 16:24],
                            smx[:, 48:56], ALU.add)
            sV.inc(v.tensor_tensor(col_sffn[:, co * 8:(co + 1) * 8], smx[:, 40:48],
                                   smx[:, 56:64], ALU.add))
        v_rt = sV.n
        sn[("p2_cols_free", c)] = sn[("p2_eexp", c, 7)]

        @vec
        def _(v, c=c):
            if ("p2_osp", c - 1) in sn:
                v.wait_ge(sL.h, sn[("p2_osp", c - 1)])
            sV.inc(v.tensor_tensor(outp[:], x2t(c % 2), hsb[:], ALU.add))
        v_outp = sV.n

        @blk.sync
        def _(s, c=c, co=co, v_outp=v_outp):
            s.wait_ge(sV.h, v_outp)
            sL.inc(s.dma_start(out=sp_outp.ap()[co], in_=outp[:]), 16)
            sL.inc(s.dma_start(out=sp_x2rfm.ap()[co], in_=x2rfm[:]), 16)
            if dbg:
                sL.inc(s.dma_start(out=sp_otile.ap()[co], in_=otile[:]), 16)
            sn[("p2_osp", c)] = sL.n
        sn[("p2_done", c)] = (sL.h, sn[("p2_osp", c)])

    p2s.close()

    phase_gate(blk)
    p3s = ExitStack()
    xin2 = p3s.enter_context(nc.sbuf_tensor("xin2", [C, 4 * HID], BF16))
    xin1 = p3s.enter_context(nc.sbuf_tensor("xin1", [C, 4 * HID], BF16))
    op4 = p3s.enter_context(nc.sbuf_tensor("op4", [C, 4 * HID], F32))
    wgu_t = p3s.enter_context(nc.sbuf_tensor("wgu", [C, 4 * 8 * INTER], BF16))
    wgu = lambda i, k: wgu_t[:, (i * 2 + k) * 8 * INTER:(i * 2 + k + 1) * 8 * INTER]
    wdn_t = p3s.enter_context(nc.sbuf_tensor("wdn", [C, 3 * 4 * HID], BF16))
    wdn = lambda i: wdn_t[:, i * 4 * HID:(i + 1) * 4 * HID]
    htm_t = p3s.enter_context(nc.sbuf_tensor("htm", [C, 4 * INTER], BF16))
    htm = lambda i: htm_t[:, i * INTER:(i + 1) * INTER]
    sgl_t = p3s.enter_context(nc.sbuf_tensor("sgl", [C, 2 * INTER], F32))
    sgl = lambda i: sgl_t[:, i * INTER:(i + 1) * INTER]
    hst_t = p3s.enter_context(nc.sbuf_tensor("hst", [C, 16 * 4 * INTER], BF16))
    hst = lambda i: hst_t[:, i * 4 * INTER:(i + 1) * 4 * INTER]
    ost_t = p3s.enter_context(nc.sbuf_tensor("ost", [C, 2 * HID], F32))
    ost = lambda i: ost_t[:, i * HID:(i + 1) * HID]

    for ti in range(8):
        @blk.sync
        def _(s, ti=ti):
            if ti == 0:
                s.wait_ge(sO.h, mst["st"][0] or 0)
                s.wait_ge(sO.h, mst["st"][1] or 0)
            if ti >= 1:
                sem, val = sn[("p3_done", ti - 1)]
                s.wait_ge(sem, val)
            sem, val = sn[("p2_done", NPRE + 4 * ti + 3)]
            s.wait_ge(sem, val)
            sL.inc(s.dma_start(out=xin2[:].rearrange("p (c f) -> p c f", c=4),
                               in_=sp_x2rfm.ap()[4 * ti:4 * ti + 4]
                                   .rearrange("c p f -> p c f")), 16)
            sL.inc(s.dma_start(out=xin1[:].rearrange("p (c f) -> p c f", c=4),
                               in_=sp_xnrfm.ap()[NPRE + 4 * ti:NPRE + 4 * ti + 4]
                                   .rearrange("c p f -> p c f")), 16)
            sL.inc(s.dma_start(out=op4[:].rearrange("p (c f) -> p c f", c=4),
                               in_=sp_outp.ap()[4 * ti:4 * ti + 4]
                                   .rearrange("c p f -> p c f")), 16)
        p3_ld = sL.n

        hps = ExitStack()
        pg = [hps.enter_context(nc.psum_tensor("p3_pg%d_%d" % (ti, i),
                                                [C, INTER], F32)) for i in range(3)]
        pu = [hps.enter_context(nc.psum_tensor("p3_pu%d_%d" % (ti, i),
                                                [C, INTER], F32)) for i in range(3)]
        pt3 = [hps.enter_context(nc.psum_tensor("p3_pt%d_%d" % (ti, i),
                                                 [C, INTER], BF16)) for i in range(2)]

        p3pend = []
        for me in range(16):
            mo, e = me // 8, me % 8
            gu_ti = (0, 1) if mo == 0 else (2, 3)
            xin = xin2 if mo == 0 else xin1
            scol = col_route if mo == 0 else col_sffn

            @blk.sync
            def _(s, ti=ti, me=me, e=e, gu_ti=gu_ti):
                if me >= 2:
                    sem, val = sn[("p3_gu_mm", ti, me - 2, 3)]
                    s.wait_ge(sem, val)
                elif ti >= 1 and me < 2:
                    sem, val = sn[("p3_gu_mm", ti - 1, 14 + me, 3)]
                    s.wait_ge(sem, val)
                sL.inc(s.dma_start(out=wgu(me % 2, 0).rearrange("p (c f) -> p c f", c=8),
                                   in_=l_gu.ap()[gu_ti[0], e]), 16)
                sL.inc(s.dma_start(out=wgu(me % 2, 1).rearrange("p (c f) -> p c f", c=8),
                                   in_=l_gu.ap()[gu_ti[1], e]), 16)
            p3_wld = sL.n

            for ccp in (0, 2):
                for cc in (ccp, ccp + 1):
                    sq = (ti * 16 + me) * 4 + cc

                    @blk.tensor
                    def _(t, ti=ti, me=me, cc=cc, sq=sq, xin=xin,
                          p3_ld=p3_ld, p3_wld=p3_wld):
                        t.wait_ge(sL.h, max(p3_ld, p3_wld))
                        if ("p3_huq", sq - 3) in sn:
                            sem, val = sn[("p3_huq", sq - 3)]
                            t.wait_ge(sem, val)
                        if me == 0 and cc == 0 and ti >= 1:
                            sem, val = sn[("p3_oev", ti - 1)]
                            t.wait_ge(sem, val)
                        mi = None
                        for d in range(8):
                            mi = t.matmul(pg[sq % 3][:],
                                          xin[:, cc * HID + d * C:cc * HID + (d + 1) * C],
                                          wgu(me % 2, 0)[:, d * INTER:(d + 1) * INTER],
                                          start=(d == 0), stop=(d == 7))
                        for d in range(8):
                            mi = t.matmul(pu[sq % 3][:],
                                          xin[:, cc * HID + d * C:cc * HID + (d + 1) * C],
                                          wgu(me % 2, 1)[:, d * INTER:(d + 1) * INTER],
                                          start=(d == 0), stop=(d == 7))
                        sT.inc(mi)
                        sn[("p3_gu_mm", ti, me, cc)] = (sT.h, sT.n)

                @blk.scalar
                def _(sc, ti=ti, me=me, mo=mo, e=e, ccp=ccp, scol=scol):
                    sem, val = sn[("p3_gu_mm", ti, me, ccp + 1)]
                    sc.wait_ge(sem, val)
                    for cc in (ccp, ccp + 1):
                        sq2 = (ti * 16 + me) * 4 + cc
                        co = 4 * ti + cc
                        if mo == 0:
                            sS.inc(sc.activation(sgl(cc % 2)[:], pg[sq2 % 3][:],
                                                 AF.Silu))
                        else:
                            sS.inc(sc.activation(sgl(cc % 2)[:], pg[sq2 % 3][:],
                                                 AF.Silu,
                                                 scale=scol[:, co * 8 + e:co * 8 + e + 1]))
                        sn[("p3_sg", ti, me, cc)] = sS.n

                for cc in (ccp, ccp + 1):
                    sq = (ti * 16 + me) * 4 + cc

                    @vec
                    def _(v, ti=ti, me=me, mo=mo, e=e, cc=cc, sq=sq, scol=scol):
                        v.wait_ge(sS.h, sn[("p3_sg", ti, me, cc)])
                        co = 4 * ti + cc
                        sV.inc(v.scalar_tensor_tensor(htm(sq % 4)[:], pu[sq % 3][:],
                                                      scol[:, co * 8 + e:co * 8 + e + 1],
                                                      sgl(cc % 2)[:],
                                                      ALU.mult, ALU.mult))
                        sn[("p3_hu", ti, me, cc)] = (sV.h, sV.n)
                        sn[("p3_huq", sq)] = (sV.h, sV.n)

                for (pme, pcc) in p3pend:
                    psq = (ti * 16 + pme) * 4 + pcc

                    @blk.tensor
                    def _(t, ti=ti, me=pme, cc=pcc, sq=psq):
                        sem, val = sn[("p3_hu", ti, me, cc)]
                        t.wait_ge(sem, val)
                        if ("p3_hevq", sq - 2) in sn:
                            sem2, v2 = sn[("p3_hevq", sq - 2)]
                            t.wait_ge(sem2, v2)
                        mi = None
                        for ii in range(4):
                            mi = t.transpose(pt3[cc % 2][:, ii * C:(ii + 1) * C],
                                             htm(sq % 4)[:, ii * C:(ii + 1) * C],
                                             ibf[:])
                        sT.inc(mi)
                        sn[("p3_htr", ti, me, cc)] = sT.n

                    @vec
                    def _(v, ti=ti, me=pme, cc=pcc, sq=psq):
                        v.wait_ge(sT.h, sn[("p3_htr", ti, me, cc)])
                        sV.inc(v.tensor_copy(hst(me)[:, cc * 512:(cc + 1) * 512],
                                             pt3[cc % 2][:]))
                        sn[("p3_hev", ti, me, cc)] = (sV.h, sV.n)
                        sn[("p3_hevq", sq)] = (sV.h, sV.n)
                p3pend[:] = [(me, ccp), (me, ccp + 1)]

        for (pme, pcc) in p3pend:
            psq = (ti * 16 + pme) * 4 + pcc

            @blk.tensor
            def _(t, ti=ti, me=pme, cc=pcc, sq=psq):
                sem, val = sn[("p3_hu", ti, me, cc)]
                t.wait_ge(sem, val)
                if ("p3_hevq", sq - 2) in sn:
                    sem2, v2 = sn[("p3_hevq", sq - 2)]
                    t.wait_ge(sem2, v2)
                mi = None
                for ii in range(4):
                    mi = t.transpose(pt3[cc % 2][:, ii * C:(ii + 1) * C],
                                     htm(sq % 4)[:, ii * C:(ii + 1) * C],
                                     ibf[:])
                sT.inc(mi)
                sn[("p3_htr", ti, me, cc)] = sT.n

            @vec
            def _(v, ti=ti, me=pme, cc=pcc, sq=psq):
                v.wait_ge(sT.h, sn[("p3_htr", ti, me, cc)])
                sV.inc(v.tensor_copy(hst(me)[:, cc * 512:(cc + 1) * 512],
                                     pt3[cc % 2][:]))
                sn[("p3_hev", ti, me, cc)] = (sV.h, sV.n)
                sn[("p3_hevq", sq)] = (sV.h, sV.n)

        hps.close()

        dps = ExitStack()
        pd = [dps.enter_context(nc.psum_tensor("p3_pd%d_%d" % (ti, i), [C, 512], F32))
              for i in range(8)]

        for me in range(16):
            mo, e = me // 8, me % 8

            @blk.sync
            def _(s, ti=ti, me=me, mo=mo, e=e):
                if me >= 3:
                    s.wait_ge(sT.h, sn[("p3_dn_mm", ti, me - 3)])
                elif ti >= 1 and me < 3:
                    s.wait_ge(sT.h, sn[("p3_dn_mm", ti - 1, 13 + me)])
                sL.inc(s.dma_start(out=wdn(me % 3).rearrange("p (c f) -> p c f", c=4),
                                   in_=l_dn.ap()[mo, e]), 16)
            p3_dld = sL.n

            @blk.tensor
            def _(t, ti=ti, me=me, p3_dld=p3_dld):
                t.wait_ge(sL.h, p3_dld)
                sem, val = sn[("p3_hev", ti, me, 3)]
                t.wait_ge(sem, val)
                if me == 0 and ti >= 1:
                    sem2, v2 = sn[("p3_oev", ti - 1)]
                    t.wait_ge(sem2, v2)
                mi = None
                for cc in range(4):
                    for dh in range(2):
                        for ii in range(4):
                            mi = t.matmul(
                                pd[cc * 2 + dh][:],
                                hst(me)[:, cc * 512 + ii * C:cc * 512 + (ii + 1) * C],
                                wdn(me % 3)[:, ii * HID + dh * 512:
                                                 ii * HID + (dh + 1) * 512],
                                start=(me == 0 and ii == 0),
                                stop=(me == 15 and ii == 3))
                sT.inc(mi)
                sn[("p3_dn_mm", ti, me)] = sT.n

        for cc in range(4):
            @vec
            def _(v, ti=ti, cc=cc):
                v.wait_ge(sT.h, sn[("p3_dn_mm", ti, 15)])
                if ("p3_ost", ti, cc - 2) in sn:
                    v.wait_ge(sL.h, sn[("p3_ost", ti, cc - 2)])
                elif cc < 2 and ti >= 1 and ("p3_ost", ti - 1, cc + 2) in sn:
                    v.wait_ge(sL.h, sn[("p3_ost", ti - 1, cc + 2)])
                mi = None
                for dh in range(2):
                    mi = v.tensor_tensor(
                        ost(cc % 2)[:, dh * 512:(dh + 1) * 512],
                        pd[cc * 2 + dh][:],
                        op4[:, cc * HID + dh * 512:cc * HID + (dh + 1) * 512], ALU.add)
                sV.inc(mi)
                sn[("p3_oevc", ti, cc)] = sV.n

            @blk.sync
            def _(s, ti=ti, cc=cc):
                s.wait_ge(sV.h, sn[("p3_oevc", ti, cc)])
                co = 4 * ti + cc
                sL.inc(s.dma_start(out=out_h.ap()[co * C:(co + 1) * C, :],
                                   in_=ost(cc % 2)), 16)
                sn[("p3_ost", ti, cc)] = sL.n

        sn[("p3_oev", ti)] = (sV.h, sn[("p3_oevc", ti, 3)])
        dps.close()
        sn[("p3_done", ti)] = (sL.h, sn[("p3_ost", ti, 3)])

    p3s.close()

    @blk.gpsimd
    def _(g):
        g.wait_ge(sO.h, sO.n)
        g.wait_ge(sL.h, sL.n)

    est.close()
    return nc


_CACHE = {}


def kernel(**inputs):
    x = np.ascontiguousarray(np.asarray(inputs["L0_attn_hidden_states"], np.float32))
    B, S, _ = x.shape
    Q = S // 4
    wmap = {
        "w_ln1": "L0_attn_w_ln1", "w_ln2": "L0_attn_w_ln2",
        "W_qkvz": "L0_attn_W_qkvz", "W_ba": "L0_attn_W_ba",
        "conv_w": "L0_attn_conv_w", "dt_bias": "L0_attn_dt_bias",
        "A_log": "L0_attn_A_log", "gate_norm_w": "L0_attn_gate_norm_w",
        "W_out": "L0_attn_W_out", "W_router": "L0_attn_W_router",
        "W_gate_e": "L0_attn_W_gate_e", "W_up_e": "L0_attn_W_up_e",
        "W_down_e": "L0_attn_W_down_e", "fgw": "L0_ffn_gate_w",
        "w1": "L0_ffn_w1", "w2": "L0_ffn_w2", "w3": "L0_ffn_w3",
    }
    ws = {k: np.ascontiguousarray(np.asarray(inputs[v], np.float32))
          for k, v in wmap.items()}
    if "nc" not in _CACHE:
        _CACHE["nc"] = build_program()
    nc = _CACHE["nc"]
    pad = NPRE * C
    in_maps = []
    for core in range(8):
        b, q = core // 4, core % 4
        lo = q * Q - pad
        if lo < 0:
            xs = np.concatenate([np.zeros((-lo, HID), np.float32),
                                 x[b, :q * Q + Q]], 0)
        else:
            xs = x[b, lo:(q + 1) * Q]
        m = {"x": np.ascontiguousarray(xs)}
        m.update(ws)
        in_maps.append(m)
    res = run_bass_kernel_spmd(nc, in_maps, core_ids=list(range(8)))
    halves = [np.concatenate([res.results[b * 4 + q]["out"] for q in range(4)], 0)
              for b in range(2)]
    return np.stack(halves, 0)


# version 16
# speedup vs baseline: 1.0146x; 1.0146x over previous
import math
from contextlib import ExitStack

import numpy as np
import ml_dtypes
import concourse.bass as bass
import concourse.mybir as mybir
from concourse.bass_utils import run_bass_kernel_spmd

F32 = mybir.dt.float32
BF16 = mybir.dt.bfloat16
AF = mybir.ActivationFunctionType
ALU = mybir.AluOpType
AX = mybir.AxisListType
BF = ml_dtypes.bfloat16

C = 128
NPRE = 5
NOWN = 32
NCH = NPRE + NOWN
TOK = NCH * C
HID = 1024
HK, HV = 4, 8
DK = DV = 128
KEY, VAL, CONVD, KC = 512, 1024, 2048, 4
NEXP, INTER = 8, 512
EPS = 1e-6
NEGB = -1.0e30
LNEPS = 1e-5
LVLS = [1, 2, 4, 8, 16, 32, 64]
LSQ = 5


class Ctr:
    def __init__(self, nc, name):
        self.h = nc.alloc_semaphore(name)
        self.n = 0

    def inc(self, inst, k=1):
        inst.then_inc(self.h, k)
        self.n += k
        return inst


class _VG:

    WIN = 3

    def __init__(self, v, jnk, hist):
        self._v = v
        self._jnk = jnk
        self._hist = hist

    @staticmethod
    def _span(ap):
        if ap is None or isinstance(ap, (int, float)):
            return None
        try:
            off = ap.offset
            sh = ap.shape
            st = ap.stride
            ext = 1
            for i in range(1, len(sh)):
                ext += (sh[i] - 1) * abs(st[i])
            return (ap.name, off, off + ext)
        except Exception:
            try:
                return (ap.name, 0, 1 << 40)
            except Exception:
                return ("?", 0, 1 << 40)

    def _op(self, writes, reads, fn):
        rs = [r for r in (self._span(a) for a in reads) if r is not None]

        def hit(ws):
            for w in ws:
                for r in rs:
                    if w[0] == r[0] and w[1] < r[2] and r[1] < w[2]:
                        return True
            return False

        h = self._hist
        while any(hit(ws) for ws in h):
            self._v.memset(self._jnk, 0.0)
            h.append([])
            del h[:-self.WIN]
        inst = fn()
        h.append([w for w in (self._span(a) for a in writes) if w is not None])
        del h[:-self.WIN]
        return inst

    def wait_ge(self, *a, **k):
        return self._v.wait_ge(*a, **k)

    def memset(self, ap, cval):
        return self._op([ap], [], lambda: self._v.memset(ap, cval))

    def tensor_copy(self, out, in_):
        return self._op([out], [in_], lambda: self._v.tensor_copy(out, in_))

    def tensor_tensor(self, out, in0, in1, op):
        return self._op([out], [in0, in1],
                        lambda: self._v.tensor_tensor(out, in0, in1, op))

    def tensor_mul(self, out, in0, in1):
        return self._op([out], [in0, in1],
                        lambda: self._v.tensor_mul(out, in0, in1))

    def tensor_scalar(self, out, in0, s1, s2, op0, op1=mybir.AluOpType.bypass,
                      accum_out=None):
        return self._op([out, accum_out], [in0, s1, s2],
                        lambda: self._v.tensor_scalar(out, in0, s1, s2, op0,
                                                      op1,
                                                      accum_out=accum_out))

    def tensor_scalar_mul(self, out, in0, s1):
        return self._op([out], [in0, s1],
                        lambda: self._v.tensor_scalar_mul(out, in0, s1))

    def tensor_scalar_add(self, out, in0, s1):
        return self._op([out], [in0, s1],
                        lambda: self._v.tensor_scalar_add(out, in0, s1))

    def tensor_scalar_sub(self, out, in0, s1):
        return self._op([out], [in0, s1],
                        lambda: self._v.tensor_scalar_sub(out, in0, s1))

    def tensor_scalar_max(self, out, in0, s1):
        return self._op([out], [in0, s1],
                        lambda: self._v.tensor_scalar_max(out, in0, s1))

    def scalar_tensor_tensor(self, out, in0, scalar, in1, op0, op1):
        return self._op([out], [in0, scalar, in1],
                        lambda: self._v.scalar_tensor_tensor(out, in0, scalar,
                                                             in1, op0, op1))

    def reduce_sum(self, out, in_, axis, **kw):
        return self._op([out], [in_],
                        lambda: self._v.reduce_sum(out, in_, axis, **kw))

    def reduce_max(self, out, in_, axis, **kw):
        return self._op([out], [in_],
                        lambda: self._v.reduce_max(out, in_, axis, **kw))

    def reciprocal(self, out, in_):
        return self._op([out], [in_], lambda: self._v.reciprocal(out, in_))


def _consts():
    i_bf = np.eye(C, dtype=BF)
    i_f = np.eye(C, dtype=np.float32)
    ones1 = np.ones((1, C), dtype=np.float32)
    ut = np.triu(np.ones((C, C), dtype=np.float32))
    m_up1 = np.where(np.triu(np.ones((C, C), bool), 1), 0.0, NEGB)
    m_lo1 = np.where(np.tril(np.ones((C, C), bool), -1), 0.0, NEGB)
    m_lo0 = np.where(np.tril(np.ones((C, C), bool), 0), 0.0, NEGB)
    masks = np.concatenate([m_lo1, m_up1, m_lo0], 1).astype(np.float32)
    r127 = np.ones((C, C), dtype=np.float32)
    shifts = np.zeros((C, KC * C), dtype=BF)
    for j in range(KC):
        for t in range(C):
            tp = t - (3 - j)
            if 0 <= tp < C:
                shifts[tp, j * C + t] = 1
    halos = np.zeros((C, 3 * C), dtype=BF)
    for j in range(3):
        for t in range(C):
            tp = t + 125 + j
            if tp < C:
                halos[tp, j * C + t] = 1
    return i_bf, i_f, ones1, ut, masks, shifts, halos, r127


def build_program(dbg=False):
    nc = bass.Bass("TRN2", target_bir_lowering=False,
                   detect_race_conditions=False)

    x_h = nc.dram_tensor("x", [TOK, HID], F32, kind="ExternalInput")
    out_h = nc.dram_tensor("out", [NOWN * C, HID], F32, kind="ExternalOutput")
    Dd = lambda n, s: nc.dram_tensor(n, s, F32, kind="ExternalInput")
    w_ln1 = Dd("w_ln1", [HID]); w_ln2 = Dd("w_ln2", [HID])
    W_qkvz = Dd("W_qkvz", [HID, 3072]); W_ba = Dd("W_ba", [HID, 16])
    conv_w = Dd("conv_w", [CONVD, KC]); dt_bias = Dd("dt_bias", [HV])
    A_log = Dd("A_log", [HV]); gnw = Dd("gate_norm_w", [DV])
    W_out = Dd("W_out", [VAL, HID]); W_router = Dd("W_router", [HID, NEXP])
    Wge = Dd("W_gate_e", [NEXP, HID, INTER]); Wue = Dd("W_up_e", [NEXP, HID, INTER])
    Wde = Dd("W_down_e", [NEXP, INTER, HID]); fgw = Dd("fgw", [HID, NEXP])
    w1 = Dd("w1", [NEXP, HID, INTER]); w2 = Dd("w2", [NEXP, INTER, HID])
    w3 = Dd("w3", [NEXP, HID, INTER])

    spk = "ExternalOutput" if dbg else "Internal"
    sp_qkvs = nc.dram_tensor("sp_qkvs", [NCH, C, CONVD], BF16, kind=spk)
    sp_sz = nc.dram_tensor("sp_sz", [NCH, C, VAL], BF16, kind=spk)
    sp_xnrfm = nc.dram_tensor("sp_xnrfm", [NCH, C, HID], BF16, kind=spk)
    sp_x2rfm = nc.dram_tensor("sp_x2rfm", [NOWN, C, HID], BF16, kind=spk)
    sp_outp = nc.dram_tensor("sp_outp", [NOWN, C, HID], F32, kind=spk)
    sp_otile = nc.dram_tensor("sp_otile", [NOWN, C, VAL], F32,
                              kind="ExternalOutput" if dbg else "Internal")
    sp_dbgf = nc.dram_tensor("sp_dbgf", [C, 256], F32,
                             kind="ExternalOutput" if dbg else "Internal")
    sp_dbgb = nc.dram_tensor("sp_dbgb", [C, 2048], BF16,
                             kind="ExternalOutput" if dbg else "Internal")
    l_gu = nc.dram_tensor("l_gu", [4, NEXP, C, 8, INTER], BF16, kind="Internal")
    l_dn = nc.dram_tensor("l_dn", [2, NEXP, C, 4, HID], BF16, kind="Internal")

    (i_bf_np, i_f_np, ones1_np, ut_np, masks_np, shifts_np, halos_np,
     r127_np) = _consts()
    c_ibf = nc.inline_tensor(i_bf_np, "c_ibf")
    c_if = nc.inline_tensor(i_f_np, "c_if")
    c_ones1 = nc.inline_tensor(ones1_np, "c_ones1")
    c_ut = nc.inline_tensor(ut_np, "c_ut")
    c_masks = nc.inline_tensor(masks_np, "c_masks")
    c_shift = nc.inline_tensor(shifts_np, "c_shift")
    c_halo = nc.inline_tensor(halos_np, "c_halo")
    c_r127 = nc.inline_tensor(r127_np, "c_r127")

    A = nc.alloc_sbuf_tensor
    ibf = A("ibf", [C, C], BF16)
    if32 = A("if32", [C, C], F32)
    ones1 = A("ones1", [1, C], F32)
    utf = A("utf", [C, C], F32)
    masks = A("masks", [C, 3 * C], F32)
    shifts = A("shifts", [C, KC * C], BF16)
    halos = A("halos", [C, 3 * C], BF16)
    r127f = A("r127f", [C, C], F32)
    col_m = A("col_m", [C, NCH], F32)
    col_rstd = A("col_rstd", [C, NCH], F32)
    col_rr = A("col_rr", [C, NCH], F32)
    col_ir1 = A("col_ir1", [C, NCH], F32)
    col_ba = A("col_ba", [C, NCH * 16], F32)
    col_ssq = A("col_ssq", [C, NCH * 8], F32)
    col_fflog = A("col_fflog", [C, NCH * 8], F32)
    col_route = A("col_route", [C, NOWN * 8], F32)
    col_sffn = A("col_sffn", [C, NOWN * 8], F32)
    ln12 = A("ln12", [C, 16], F32)
    gnw_c = A("gnw_c", [C, 1], F32)
    expA_b = A("expA_b", [C, 8], F32)
    dt_b = A("dt_b", [C, 8], F32)
    wba_bf = A("wba_bf", [C, 8 * 16], BF16)
    wrt_bf = A("wrt_bf", [C, 8 * NEXP], BF16)
    fgw_bf = A("fgw_bf", [C, 8 * NEXP], BF16)
    s_state = A("s_state", [C, 2 * HV * DV], BF16)
    rows0 = A("rows0", [1, 16 + CONVD], F32)
    cst = A("cst", [C, 12], F32)
    vjnk = A("vjnk", [C, 8], F32)
    CVALS = [LNEPS, 1e-11, EPS, 1.0 / HID, (1.0 + EPS) / HID, -0.5, 1.0 / DV,
             -0.5 * math.log(128.0), NEGB, -1.0, 0.0, 1.0]
    for ci, cv in enumerate(CVALS):
        nc.const_aps.register(F32, cv, cst[:, ci:ci + 1])

    sL = Ctr(nc, "sL"); sG = Ctr(nc, "sG"); sT = Ctr(nc, "sT")
    sV = Ctr(nc, "sV"); sS = Ctr(nc, "sS"); sO = Ctr(nc, "sO")
    sn = {}

    def phase_gate(blk_):
        gates = [(sL, "sync"), (sG, "gpsimd"), (sT, "tensor"),
                 (sV, "vector"), (sS, "scalar"), (sO, "stores")]
        vals = {id(ct): ct.n for ct, _ in gates}

        def mk(eng_name):
            def f(eng):
                for ct, nm in gates:
                    if nm != eng_name and vals[id(ct)] > 0:
                        eng.wait_ge(ct.h, vals[id(ct)])
            return f
        blk_.sync(mk("sync"))
        blk_.gpsimd(mk("gpsimd"))
        blk_.tensor(mk("tensor"))
        blk_.vector(mk("vector"))
        blk_.scalar(mk("scalar"))

    est = ExitStack()
    blk = est.enter_context(nc.Block())

    _vhist = []

    def vec(fn):
        return blk.vector(lambda v: fn(_VG(v, vjnk[:, 0:8], _vhist)))
    p01s = ExitStack()
    wq_bf = p01s.enter_context(nc.sbuf_tensor("wq_bf", [C, 8 * 3072], BF16))
    wjb = p01s.enter_context(nc.sbuf_tensor("wjb", [C, KC * CONVD], BF16))
    b0a = p01s.enter_context(nc.sbuf_tensor("p0_b1", [C, 2048], BF16))
    b0b = p01s.enter_context(nc.sbuf_tensor("p0_b2", [C, 2048], BF16))

    p0s = ExitStack()
    stg = p0s.enter_context(nc.sbuf_tensor("p0_stg", [C, 3072], F32))
    p0ps = p0s.enter_context(nc.psum_tensor("p0_ps", [C, 512], F32))

    @blk.sync
    def _(s):
        sL.inc(s.dma_start(out=ibf[:], in_=c_ibf[:, :]), 16)
        sL.inc(s.dma_start(out=if32[:], in_=c_if[:, :]), 16)
        sL.inc(s.dma_start(out=ones1[:], in_=c_ones1[:, :]), 16)
        sL.inc(s.dma_start(out=utf[:], in_=c_ut[:, :]), 16)
        sL.inc(s.dma_start(out=masks[:], in_=c_masks[:, :]), 16)
        sL.inc(s.dma_start(out=shifts[:], in_=c_shift[:, :]), 16)
        sL.inc(s.dma_start(out=halos[:], in_=c_halo[:, :]), 16)
        sL.inc(s.dma_start(out=r127f[:], in_=c_r127[:, :]), 16)
        sL.inc(s.dma_start(out=ln12[:, 0:8],
                           in_=w_ln1.ap().rearrange("(t p) -> p t", p=C),
                           allow_slow_non_contiguous=True), 16)
        sL.inc(s.dma_start(out=ln12[:, 8:16],
                           in_=w_ln2.ap().rearrange("(t p) -> p t", p=C),
                           allow_slow_non_contiguous=True), 16)
        sL.inc(s.dma_start(out=gnw_c[:, 0:1],
                           in_=gnw.ap()[:, None],
                           allow_slow_non_contiguous=True), 16)
        sL.inc(s.dma_start(out=rows0[0:1, 0:8], in_=dt_bias.ap()[None, :]), 16)
        sL.inc(s.dma_start(out=rows0[0:1, 8:16], in_=A_log.ap()[None, :]), 16)

    ld_consts = sL.n

    @blk.gpsimd
    def _(g):
        for d in range(8):
            sG.inc(g.dma_start(out=wba_bf[:, d * 16:(d + 1) * 16],
                               in_=W_ba.ap()[d * C:(d + 1) * C, :]), 16)
            sG.inc(g.dma_start(out=wrt_bf[:, d * 8:(d + 1) * 8],
                               in_=W_router.ap()[d * C:(d + 1) * C, :]), 16)
            sG.inc(g.dma_start(out=fgw_bf[:, d * 8:(d + 1) * 8],
                               in_=fgw.ap()[d * C:(d + 1) * C, :]), 16)
    gl_small = sG.n

    @vec
    def _(v):
        v.wait_ge(sL.h, ld_consts)
        for ci, cv in enumerate(CVALS):
            v.memset(cst[:, ci:ci + 1], cv)
        v.memset(s_state[:], 0.0)
        sV.inc(v.tensor_scalar_add(ln12[:], ln12[:], cst[:, 11:12]))
    v_ln = sV.n

    @blk.tensor
    def _(t):
        t.wait_ge(sL.h, ld_consts)
        t.matmul(p0ps[:, 0:8], ones1[0:1, :], rows0[0:1, 8:16], start=True, stop=False)
        sT.inc(t.matmul(p0ps[:, 8:16], ones1[0:1, :], rows0[0:1, 0:8],
                        start=False, stop=True))
    t_adt = sT.n

    @blk.scalar
    def _(sc):
        sc.wait_ge(sT.h, t_adt)
        sc.wait_ge(sV.h, v_ln)
        sc.activation(expA_b[:], p0ps[:, 0:8], AF.Exp)
        sS.inc(sc.copy(dt_b[:], p0ps[:, 8:16]))

    for j in range(KC):
        @blk.sync
        def _(s, j=j):
            if j >= 1:
                s.wait_ge(sT.h, sT.n)
            sL.inc(s.dma_start(out=rows0[0:1, 16:16 + CONVD],
                               in_=conv_w.ap()[:, j][None, :],
                               allow_slow_non_contiguous=True), 16)
        for gq in range(4):
            @blk.tensor
            def _(t, j=j, gq=gq):
                t.wait_ge(sL.h, sL.n)
                t.wait_ge(sS.h, sS.n)
                sT.inc(t.matmul(p0ps[:], ones1[0:1, :],
                                rows0[0:1, 16 + gq * 512:16 + (gq + 1) * 512],
                                start=True, stop=True))

            @blk.scalar
            def _(sc, j=j, gq=gq):
                sc.wait_ge(sT.h, sT.n)
                sS.inc(sc.copy(wjb[:, j * CONVD + gq * 512:j * CONVD + (gq + 1) * 512],
                               p0ps[:]))

    for d in range(8):
        @blk.sync
        def _(s, d=d):
            if d >= 1:
                s.wait_ge(sV.h, v_ln + d)
            sL.inc(s.dma_start(out=stg[:], in_=W_qkvz.ap()[d * C:(d + 1) * C, :]), 16)

        @vec
        def _(v, d=d):
            v.wait_ge(sL.h, sL.n)
            sV.inc(v.tensor_scalar_mul(wq_bf[:, d * 3072:(d + 1) * 3072],
                                       stg[:], ln12[:, d:d + 1]))
    v_wq = sV.n

    for d in range(8):
        @vec
        def _(v, d=d):
            if d == 0:
                v.wait_ge(sG.h, gl_small)
            sV.inc(v.tensor_scalar_mul(wba_bf[:, d * 16:(d + 1) * 16],
                                       wba_bf[:, d * 16:(d + 1) * 16],
                                       ln12[:, d:d + 1]))
    v_wba = sV.n
    p0s.close()

    moe_jobs = []
    for ti, th in enumerate([Wge, Wue, w1, w3]):
        for e in range(NEXP):
            for dh in range(2):
                moe_jobs.append(("gu", ti, e, dh, th))
    for ti, th in enumerate([Wde, w2]):
        for e in range(NEXP):
            for dh in range(2):
                moe_jobs.append(("dn", ti, e, dh, th))
    mst = {"i": 0, "bufs": [b0a, b0b], "st": [None, None], "final": None}

    def emit_moe_jobs(k):
        for _ in range(k):
            i = mst["i"]
            if i >= len(moe_jobs):
                mst["final"] = mst["final"] or sO.n
                return
            mst["i"] = i + 1
            kind, ti, e, dh, th = moe_jobs[i]
            buf = mst["bufs"][i % 2]
            prev_st = mst["st"][i % 2]

            @blk.gpsimd
            def _(g, kind=kind, ti=ti, e=e, dh=dh, th=th, buf=buf, prev_st=prev_st):
                if prev_st is not None:
                    g.wait_ge(sO.h, prev_st)
                if kind == "gu":
                    sG.inc(g.dma_start(
                        out=buf[:].rearrange("p (c f) -> p c f", c=4),
                        in_=th.ap()[e, dh * 512:(dh + 1) * 512, :]
                            .rearrange("(c p) f -> p c f", p=C)), 16)
                else:
                    sG.inc(g.dma_start(
                        out=buf[:].rearrange("p (c f) -> p c f", c=2),
                        in_=th.ap()[e, dh * 256:(dh + 1) * 256, :]
                            .rearrange("(c p) f -> p c f", p=C)), 16)
            gl = sG.n

            @blk.sync
            def _(s, kind=kind, ti=ti, e=e, dh=dh, buf=buf, gl=gl):
                s.wait_ge(sG.h, gl)
                if kind == "gu":
                    sO.inc(s.dma_start(
                        out=l_gu.ap()[ti, e, :, dh * 4:(dh + 1) * 4, :],
                        in_=buf[:].rearrange("p (c f) -> p c f", c=4)), 16)
                else:
                    sO.inc(s.dma_start(
                        out=l_dn.ap()[ti, e, :, dh * 2:(dh + 1) * 2, :],
                        in_=buf[:].rearrange("p (c f) -> p c f", c=2)), 16)
            mst["st"][i % 2] = sO.n

    phase_gate(blk)
    p1as = ExitStack()
    xa_t = p1as.enter_context(nc.sbuf_tensor("xa", [C, 2 * HID], F32))
    xa = lambda i: xa_t[:, i * HID:(i + 1) * HID]
    scr1 = p1as.enter_context(nc.sbuf_tensor("p1a_scr", [C, HID], F32))
    ccol = p1as.enter_context(nc.sbuf_tensor("p1a_col", [C, 8], F32))

    for c in range(NCH):
        @blk.sync
        def _(s, c=c):
            if c >= 2:
                s.wait_ge(sV.h, sn[("a_rel", c - 2)])
            sL.inc(s.dma_start(out=xa(c % 2), in_=x_h.ap()[c * C:(c + 1) * C, :]), 16)

        @vec
        def _(v, c=c):
            v.wait_ge(sL.h, sL.n)
            if c == 0:
                v.wait_ge(sV.h, v_wba)
            if c >= 1:
                v.wait_ge(sS.h, sS.n)
            v.reduce_sum(ccol[:, 0:1], xa(c % 2), axis=AX.X)
            v.tensor_scalar_mul(col_m[:, c:c + 1], ccol[:, 0:1], cst[:, 3:4])
            sV.inc(v.tensor_scalar_sub(scr1[:], xa(c % 2), col_m[:, c:c + 1]))
            sn[("a_rel", c)] = sV.n

        @blk.scalar
        def _(sc, c=c):
            sc.wait_ge(sV.h, sV.n)
            sc.activation(scr1[:], scr1[:], AF.Square, accum_out=ccol[:, 1:2])
            sc.copy(ccol[:, 6:7], ccol[:, 0:1])
            sc.activation(ccol[:, 2:3], ccol[:, 1:2], AF.Ln, bias=LNEPS, scale=1.0 / HID)
            sc.activation(ccol[:, 3:4], ccol[:, 1:2], AF.Ln,
                          bias=1e-11, scale=(1.0 + EPS) / HID)
            sc.activation(col_rstd[:, c:c + 1], ccol[:, 2:3], AF.Exp, scale=-0.5)
            sS.inc(sc.activation(col_rr[:, c:c + 1], ccol[:, 3:4], AF.Exp, scale=-0.5))

        @vec
        def _(v, c=c):
            v.wait_ge(sS.h, sS.n)
            v.reciprocal(ccol[:, 4:5], col_rr[:, c:c + 1])
            v.tensor_copy(ccol[:, 5:6], col_rstd[:, c:c + 1])
            sV.inc(v.tensor_mul(col_ir1[:, c:c + 1], ccol[:, 4:5], ccol[:, 5:6]))
    p1as.close()

    phase_gate(blk)
    p1bs = ExitStack()
    xb1 = p1bs.enter_context(nc.sbuf_tensor("xb1", [C, HID], F32))
    xnr2 = p1bs.enter_context(nc.sbuf_tensor("xnr2", [C, HID], BF16))
    xnrfm_t = p1bs.enter_context(nc.sbuf_tensor("xnrfm", [C, 2 * HID], BF16))
    xnrfm = lambda i: xnrfm_t[:, i * HID:(i + 1) * HID]
    pre_t = p1bs.enter_context(nc.sbuf_tensor("pre", [C, 2 * KC * CONVD], BF16))
    pre = lambda s_, j_: pre_t[:, (s_ * KC + j_) * CONVD:(s_ * KC + j_ + 1) * CONVD]
    qkvs_t = p1bs.enter_context(nc.sbuf_tensor("qkvs_t", [C, CONVD], BF16))
    sz_t = p1bs.enter_context(nc.sbuf_tensor("sz_t", [C, VAL], BF16))
    l2scr = p1bs.enter_context(nc.sbuf_tensor("l2scr", [C, C], F32))
    pq0 = p1bs.enter_context(nc.psum_tensor("pq0", [C, 512], F32))
    pq1 = p1bs.enter_context(nc.psum_tensor("pq1", [C, 512], F32))
    pc0 = p1bs.enter_context(nc.psum_tensor("pc0", [C, 512], F32))
    pc1 = p1bs.enter_context(nc.psum_tensor("pc1", [C, 512], F32))
    pt1b = p1bs.enter_context(nc.psum_tensor("pt1b", [C, HID], BF16))
    pl1b = p1bs.enter_context(nc.psum_tensor("pl1b", [C, 32], F32))
    pq = [pq0, pq1]
    pcv = [pc0, pc1]

    for c in range(NCH):
        emit_moe_jobs(3)

        @blk.sync
        def _(s, c=c):
            if c >= 1:
                s.wait_ge(sV.h, sn[("b_xrel", c - 1)])
            if c == 0:
                s.wait_ge(sV.h, v_wq)
            sL.inc(s.dma_start(out=xb1[:], in_=x_h.ap()[c * C:(c + 1) * C, :]), 16)

        @vec
        def _(v, c=c):
            v.wait_ge(sL.h, sL.n)
            if c >= 1:
                _sem, _val = sn[("b_fmdone", c - 1)]
                v.wait_ge(_sem, _val)
            sV.inc(v.tensor_scalar(xnr2[:], xb1[:],
                                   col_m[:, c:c + 1], col_rr[:, c:c + 1],
                                   ALU.subtract, ALU.mult))
            sn[("b_xrel", c)] = sV.n

        @blk.tensor
        def _(t, c=c):
            t.wait_ge(sV.h, sV.n)
            if c >= 1:
                _sem, _val = sn[("b_fmdone", c - 1)]
                t.wait_ge(_sem, _val)
            mi = None
            for k in range(8):
                mi = t.transpose(pt1b[:, k * C:(k + 1) * C],
                                 xnr2[:, k * C:(k + 1) * C], ibf[:])
            sT.inc(mi)
            sn[("b_tr", c)] = sT.n

        @vec
        def _(v, c=c):
            v.wait_ge(sT.h, sn[("b_tr", c)])
            if c >= 2:
                v.wait_ge(sL.h, sn[("b_fmsp", c - 2)])
            sV.inc(v.tensor_copy(xnrfm(c % 2), pt1b[:]))
            sn[("b_fmdone", c)] = (sV.h, sV.n)

        @blk.sync
        def _(s, c=c):
            sem, val = sn[("b_fmdone", c)]
            s.wait_ge(sem, val)
            sL.inc(s.dma_start(out=sp_xnrfm.ap()[c], in_=xnrfm(c % 2)), 16)
            sn[("b_fmsp", c)] = sL.n

        for fg in range(6):
            @blk.tensor
            def _(t, c=c, fg=fg):
                if fg == 0:
                    sem, val = sn[("b_fmdone", c)]
                    t.wait_ge(sem, val)
                key = ("b_qd", c, fg - 2) if fg >= 2 else ("b_qd", c - 1, fg + 4)
                if key in sn:
                    sem, val = sn[key]
                    t.wait_ge(sem, val)
                mi = None
                for d in range(8):
                    mi = t.matmul(pq[fg % 2][:],
                                  xnrfm(c % 2)[:, d * C:(d + 1) * C],
                                  wq_bf[:, d * 3072 + fg * 512:d * 3072 + (fg + 1) * 512],
                                  start=(d == 0), stop=(d == 7))
                sT.inc(mi)
                sn[("b_qmm", c, fg)] = sT.n

            if fg < 4:
                @vec
                def _(v, c=c, fg=fg):
                    v.wait_ge(sT.h, sn[("b_qmm", c, fg)])
                    if fg == 0 and ("b_convdone", c - 1) in sn:
                        v.wait_ge(sT.h, sn[("b_convdone", c - 1)])
                    mi = None
                    for j in range(KC):
                        mi = v.tensor_tensor(
                            pre(c % 2, j)[:, fg * 512:(fg + 1) * 512],
                            pq[fg % 2][:],
                            wjb[:, j * CONVD + fg * 512:j * CONVD + (fg + 1) * 512],
                            ALU.mult)
                    sV.inc(mi)
                    sn[("b_qd", c, fg)] = (sV.h, sV.n)
            else:
                @blk.scalar
                def _(sc, c=c, fg=fg):
                    sc.wait_ge(sT.h, sn[("b_qmm", c, fg)])
                    if c >= 1 and fg == 4:
                        sc.wait_ge(sL.h, sn[("b_spilled", c - 1)])
                    zg = fg - 4
                    sS.inc(sc.activation(sz_t[:, zg * 512:(zg + 1) * 512],
                                         pq[fg % 2][:], AF.Silu))
                    sn[("b_qd", c, fg)] = (sS.h, sS.n)

        @blk.tensor
        def _(t, c=c):
            if c >= 1:
                sem, val = sn[("b_badone", c - 1)]
                t.wait_ge(sem, val)
            for d in range(8):
                t.matmul(pl1b[:, 0:16], xnrfm(c % 2)[:, d * C:(d + 1) * C],
                         wba_bf[:, d * 16:(d + 1) * 16], start=(d == 0), stop=(d == 7))
            mi = None
            for d in range(8):
                mi = t.matmul(pl1b[:, 16:24], xnrfm(c % 2)[:, d * C:(d + 1) * C],
                              fgw_bf[:, d * 8:(d + 1) * 8], start=(d == 0), stop=(d == 7))
            sT.inc(mi)
        t_ba = sT.n

        @vec
        def _(v, c=c, t_ba=t_ba):
            v.wait_ge(sT.h, t_ba)
            v.tensor_copy(col_ba[:, c * 16:(c + 1) * 16], pl1b[:, 0:16])
            sV.inc(v.tensor_scalar_mul(col_fflog[:, c * 8:(c + 1) * 8],
                                       pl1b[:, 16:24], col_ir1[:, c:c + 1]))
            sn[("b_badone", c)] = (sV.h, sV.n)

        if c >= 1:
            for fgp in (0, 2):
                for fg in (fgp, fgp + 1):
                    @blk.tensor
                    def _(t, c=c, fg=fg):
                        sem, val = sn[("b_qd", c, 3)]
                        t.wait_ge(sem, val)
                        key = ("b_convd", c, fg - 2) if fg >= 2 else \
                            ("b_convd", c - 1, fg + 2)
                        if key in sn:
                            sem2, v2 = sn[key]
                            t.wait_ge(sem2, v2)
                        sl = slice(fg * 512, (fg + 1) * 512)
                        for j in range(KC):
                            t.matmul(pcv[fg % 2][:], shifts[:, j * C:(j + 1) * C],
                                     pre(c % 2, j)[:, sl], start=(j == 0), stop=False)
                        mi = None
                        for j in range(3):
                            mi = t.matmul(pcv[fg % 2][:], halos[:, j * C:(j + 1) * C],
                                          pre((c - 1) % 2, j)[:, sl],
                                          start=False, stop=(j == 2))
                        sT.inc(mi)
                        sn[("b_convmm", c, fg)] = sT.n
                        if fg == 3:
                            sn[("b_convdone", c)] = sT.n

                @blk.scalar
                def _(sc, c=c, fgp=fgp):
                    sc.wait_ge(sT.h, sn[("b_convmm", c, fgp + 1)])
                    if fgp == 0 and c >= 2:
                        sc.wait_ge(sL.h, sn[("b_spilled", c - 1)])
                    for fg in (fgp, fgp + 1):
                        sS.inc(sc.activation(qkvs_t[:, fg * 512:(fg + 1) * 512],
                                             pcv[fg % 2][:], AF.Silu))
                        sn[("b_convd", c, fg)] = (sS.h, sS.n)
                    if fgp == 2:
                        mi = None
                        for jj in range(8):
                            mi = sc.activation(
                                l2scr[:], qkvs_t[:, jj * C:(jj + 1) * C],
                                AF.Square,
                                accum_out=col_ssq[:, c * 8 + jj:c * 8 + jj + 1])
                        sS.inc(mi)
            s_l2 = sS.n

            @blk.sync
            def _(s, c=c, s_l2=s_l2):
                s.wait_ge(sS.h, s_l2)
                sL.inc(s.dma_start(out=sp_qkvs.ap()[c], in_=qkvs_t[:]), 16)
                sem, val = sn[("b_qd", c, 5)]
                s.wait_ge(sem, val)
                sL.inc(s.dma_start(out=sp_sz.ap()[c], in_=sz_t[:]), 16)
                sn[("b_spilled", c)] = sL.n
        else:
            @blk.sync
            def _(s, c=c):
                sem, val = sn[("b_qd", c, 5)]
                s.wait_ge(sem, val)
                sL.inc(s.dma_start(out=sp_sz.ap()[c], in_=sz_t[:]), 16)
                sn[("b_spilled", c)] = sL.n

    p1bs.close()
    emit_moe_jobs(len(moe_jobs) + 1)
    p01s.close()

    phase_gate(blk)
    p2s = ExitStack()
    wout_bf = p2s.enter_context(nc.sbuf_tensor("wout_bf", [C, 8 * HID], BF16))
    qkvs2_t = p2s.enter_context(nc.sbuf_tensor("qkvs2", [C, 2 * CONVD], BF16))
    qkvs2 = lambda i, _w=CONVD: qkvs2_t[:, i * _w:(i + 1) * _w]
    szf_t = p2s.enter_context(nc.sbuf_tensor("szf", [C, 2 * VAL], F32))
    szf = lambda i, _w=VAL: szf_t[:, i * _w:(i + 1) * _w]
    x2t_t = p2s.enter_context(nc.sbuf_tensor("x2t", [C, 2 * HID], F32))
    x2t = lambda i, _w=HID: x2t_t[:, i * _w:(i + 1) * _w]
    kqfm_t = p2s.enter_context(nc.sbuf_tensor("kqfm", [C, 2 * HID], BF16))
    kqfm = lambda i, _w=HID: kqfm_t[:, i * _w:(i + 1) * _w]
    ebuf_t = p2s.enter_context(nc.sbuf_tensor("ebuf", [C, 2 * 384], F32))
    ebuf = lambda i, _w=384: ebuf_t[:, i * _w:(i + 1) * _w]
    colsb = p2s.enter_context(nc.sbuf_tensor("colsb", [C, 64], F32))
    gtile = p2s.enter_context(nc.sbuf_tensor("gtile", [C, 8], F32))
    gcb = p2s.enter_context(nc.sbuf_tensor("gcb", [C, 8], F32))
    ecb = p2s.enter_context(nc.sbuf_tensor("ecb", [C, 8], F32))
    ecolK = p2s.enter_context(nc.sbuf_tensor("ecolK", [C, 8], F32))
    colA = p2s.enter_context(nc.sbuf_tensor("colA", [C, 8], F32))
    colSB = p2s.enter_context(nc.sbuf_tensor("colSB", [C, 8], F32))
    colQ = p2s.enter_context(nc.sbuf_tensor("colQ", [C, 8], F32))
    scr8 = p2s.enter_context(nc.sbuf_tensor("scr8", [C, 48], F32))
    dpre = p2s.enter_context(nc.sbuf_tensor("dpre", [C, 8 * 384], F32))
    nn_t = p2s.enter_context(nc.sbuf_tensor("nn", [C, 2 * 2 * (HV * DK)], BF16))
    nbh = lambda i, h: nn_t[:, i * 2048 + h * 256:i * 2048 + h * 256 + 128]
    ntbh = lambda i, h: nn_t[:, i * 2048 + h * 256 + 128:i * 2048 + (h + 1) * 256]
    nnh = lambda i, h: nn_t[:, i * 2048 + h * 256:i * 2048 + (h + 1) * 256]
    rbb_t = p2s.enter_context(nc.sbuf_tensor("rbb", [C, 2 * (HV * 256)], BF16))
    rbb = lambda i, _w=HV * 256: rbb_t[:, i * _w:(i + 1) * _w]
    wu2 = p2s.enter_context(nc.sbuf_tensor("wu2", [C, HV * 256], BF16))
    kpp = p2s.enter_context(nc.sbuf_tensor("kpp", [C, HV * DK], BF16))
    mtb = p2s.enter_context(nc.sbuf_tensor("mtb", [C, HV * DK], BF16))
    mub = p2s.enter_context(nc.sbuf_tensor("mub", [C, HV * DK], BF16))
    ztb = p2s.enter_context(nc.sbuf_tensor("ztb", [C, HV * DK], BF16))
    ptb = p2s.enter_context(nc.sbuf_tensor("ptb", [C, HV * DK], BF16))
    qtt = p2s.enter_context(nc.sbuf_tensor("qtt", [C, DK], BF16))
    qtfm = p2s.enter_context(nc.sbuf_tensor("qtfm", [C, HV * DK], BF16))
    otile = p2s.enter_context(nc.sbuf_tensor("otile", [C, VAL], F32))
    og = p2s.enter_context(nc.sbuf_tensor("og", [C, VAL], BF16))
    ogfm = p2s.enter_context(nc.sbuf_tensor("ogfm", [C, VAL], BF16))
    hsb = p2s.enter_context(nc.sbuf_tensor("hsb", [C, HID], F32))
    xnsb = p2s.enter_context(nc.sbuf_tensor("xnsb", [C, HID], F32))
    x2rt = p2s.enter_context(nc.sbuf_tensor("x2rt", [C, HID], BF16))
    x2rfm = p2s.enter_context(nc.sbuf_tensor("x2rfm", [C, HID], BF16))
    outp = p2s.enter_context(nc.sbuf_tensor("outp", [C, HID], F32))
    rs8 = p2s.enter_context(nc.sbuf_tensor("rs8", [C, 8], F32))
    smx = p2s.enter_context(nc.sbuf_tensor("smx", [C, 64], F32))
    rm8 = p2s.enter_context(nc.sbuf_tensor("rm8", [C, 8], F32))

    @vec
    def _(v):
        sV.inc(v.memset(rm8[:], 0.0))
    sqscr = p2s.enter_context(nc.sbuf_tensor("sqscr", [C, 1024], F32))
    p_kk = p2s.enter_context(nc.psum_tensor("p_kk", [C, 512], F32))
    p_qk = p2s.enter_context(nc.psum_tensor("p_qk", [C, 512], F32))
    p_r0 = p2s.enter_context(nc.psum_tensor("p_r0", [C, 384], F32))
    p_r1 = p2s.enter_context(nc.psum_tensor("p_r1", [C, 384], F32))
    p_s0 = p2s.enter_context(nc.psum_tensor("p_s0", [C, 384], F32))
    p_s1 = p2s.enter_context(nc.psum_tensor("p_s1", [C, 384], F32))
    p_tb = p2s.enter_context(nc.psum_tensor("p_tb", [C, HID], BF16))
    p_sm = p2s.enter_context(nc.psum_tensor("p_sm", [C, 256], F32))
    p_r = [p_r0, p_r1]
    p_sx = [p_s0, p_s1]

    def khd(h):
        return slice(h * DK, (h + 1) * DK)

    @blk.gpsimd
    def _(g):
        for d in range(8):
            sG.inc(g.dma_start(out=wout_bf[:, d * HID:(d + 1) * HID],
                               in_=W_out.ap()[d * C:(d + 1) * C, :]), 16)
    gl_wout = sG.n

    for c in range(1, NCH):
        own = c >= NPRE
        co = c - NPRE

        @blk.sync
        def _(s, c=c, own=own):
            if ("p2_done", c - 2) in sn:
                sem, val = sn[("p2_done", c - 2)]
                s.wait_ge(sem, val)
            s.wait_ge(sL.h, sn[("b_spilled", c)])
            sL.inc(s.dma_start(out=qkvs2(c % 2), in_=sp_qkvs.ap()[c]), 16)
            if own:
                sL.inc(s.dma_start(out=x2t(c % 2),
                                   in_=x_h.ap()[c * C:(c + 1) * C, :]), 16)
        p2_ld = sL.n

        if own:
            @blk.gpsimd
            def _(g, c=c):
                if ("p2_done", c - 2) in sn:
                    sem, val = sn[("p2_done", c - 2)]
                    g.wait_ge(sem, val)
                sG.inc(g.dma_start(out=szf(c % 2), in_=sp_sz.ap()[c]), 16)
        p2_gld = sG.n


        @vec
        def _(v, c=c):
            if ("p2_cols_free", c - 1) in sn:
                sem, val = sn[("p2_cols_free", c - 1)]
                v.wait_ge(sem, val)
            if ("p2_gcb_t", c - 1) in sn:
                v.wait_ge(sT.h, sn[("p2_gcb_t", c - 1)])
            if ("p2_dump", c - 1) in sn:
                v.wait_ge(sL.h, sn[("p2_dump", c - 1)])
            v.tensor_tensor(scr8[:, 0:8], col_ba[:, c * 16 + 8:c * 16 + 16],
                            dt_b[:], ALU.add)
            sV.inc(v.tensor_scalar_mul(scr8[:, 32:40],
                                       col_ba[:, c * 16:c * 16 + 8],
                                       cst[:, 9:10]))

        @blk.scalar
        def _(sc, c=c):
            sc.wait_ge(sV.h, sV.n)
            sc.activation(scr8[:, 8:16], scr8[:, 0:8], AF.Abs)
            sc.activation(scr8[:, 8:16], scr8[:, 8:16], AF.Exp, scale=-1.0)
            sc.activation(scr8[:, 8:16], scr8[:, 8:16], AF.Ln, bias=1.0)
            sc.activation(scr8[:, 40:48], scr8[:, 32:40], AF.Abs)
            sc.activation(scr8[:, 40:48], scr8[:, 40:48], AF.Exp, scale=-1.0)
            sc.activation(scr8[:, 40:48], scr8[:, 40:48], AF.Ln, bias=1.0)
            sc.activation(scr8[:, 24:28], col_ssq[:, c * 8 + 4:c * 8 + 8],
                          AF.Ln, bias=EPS)
            sS.inc(sc.activation(scr8[:, 28:32], col_ssq[:, c * 8:c * 8 + 4],
                                 AF.Ln, bias=EPS))

        @vec
        def _(v, c=c):
            v.wait_ge(sS.h, sS.n)
            v.tensor_scalar_max(scr8[:, 16:24], scr8[:, 0:8], cst[:, 10:11])
            v.tensor_tensor(scr8[:, 16:24], scr8[:, 16:24], scr8[:, 8:16],
                            ALU.add)
            sV.inc(v.scalar_tensor_tensor(gtile[:], scr8[:, 16:24],
                                          cst[:, 9:10], expA_b[:],
                                          ALU.mult, ALU.mult))
        v_gt = sV.n

        @blk.tensor
        def _(t, c=c, v_gt=v_gt):
            t.wait_ge(sV.h, v_gt)
            if ("p2_psm_free", c - 1) in sn:
                sem, val = sn[("p2_psm_free", c - 1)]
                t.wait_ge(sem, val)
            t.matmul(p_sm[:, 0:8], utf[:], gtile[:], start=True, stop=True)
            sT.inc(t.matmul(p_sm[:, 8:16], r127f[:], gtile[:],
                            start=True, stop=True))
        t_g = sT.n
        sn[("p2_gcb_t", c)] = t_g

        @vec
        def _(v, c=c, own=own, t_g=t_g):
            v.wait_ge(sT.h, t_g)
            v.tensor_scalar_max(scr8[:, 16:24], scr8[:, 32:40], cst[:, 10:11])
            v.tensor_tensor(scr8[:, 16:24], scr8[:, 16:24], scr8[:, 40:48],
                            ALU.add)
            for h in range(8):
                v.tensor_scalar_mul(colsb[:, 48 + h:49 + h],
                                    scr8[:, 24 + h // 2:25 + h // 2],
                                    cst[:, 5:6])
            v.tensor_scalar_mul(colsb[:, 40:48], scr8[:, 16:24], cst[:, 5:6])
            if own:
                for h in range(8):
                    v.tensor_scalar(colsb[:, 56 + h:57 + h],
                                    scr8[:, 28 + h // 2:29 + h // 2],
                                    cst[:, 5:6], cst[:, 7:8],
                                    ALU.mult, ALU.add)
            v.tensor_copy(gcb[:], p_sm[:, 8:16])
            v.tensor_copy(colsb[:, 32:40], p_sm[:, 0:8])
            v.tensor_tensor(scr8[:, 8:16], colsb[:, 40:48], colsb[:, 48:56],
                            ALU.add)
            v.tensor_tensor(colsb[:, 0:8], p_sm[:, 0:8], scr8[:, 8:16],
                            ALU.add)
            v.tensor_tensor(colsb[:, 8:16], scr8[:, 8:16], p_sm[:, 0:8],
                            ALU.subtract)
            if own:
                v.tensor_tensor(colsb[:, 16:24], p_sm[:, 0:8],
                                colsb[:, 56:64], ALU.add)
            v.tensor_tensor(colsb[:, 24:32], colsb[:, 48:56], p_sm[:, 0:8],
                            ALU.subtract)
            sV.inc(v.tensor_tensor(scr8[:, 8:16], colsb[:, 24:32], gcb[:],
                                   ALU.add))
        v_cab = sV.n
        v_cols1 = v_cab

        @blk.scalar
        def _(sc, c=c, own=own, v_cab=v_cab):
            sc.wait_ge(sV.h, v_cab)
            sc.activation(ecb[:], gcb[:], AF.Exp)
            sc.activation(ecolK[:], scr8[:, 8:16], AF.Exp)
            sc.activation(colA[:], colsb[:, 0:8], AF.Exp)
            mi = sc.activation(colSB[:], colsb[:, 40:48], AF.Exp)
            if own:
                mi = sc.activation(colQ[:], colsb[:, 16:24], AF.Exp)
            sS.inc(mi)
        s_cols = sS.n

        sn[("p2_psm_free", c)] = (sV.h, v_cab)
        v_rows = v_cab

        @blk.tensor
        def _(t, c=c, own=own, p2_ld=p2_ld):
            t.wait_ge(sL.h, p2_ld)
            if ("p2_ptb_free", c - 1) in sn:
                sem, val = sn[("p2_ptb_free", c - 1)]
                t.wait_ge(sem, val)
            mi = None
            for j in range(4):
                mi = t.transpose(p_tb[:, j * C:(j + 1) * C],
                                 qkvs2(c % 2)[:, KEY + j * C:KEY + (j + 1) * C],
                                 ibf[:])
            if own:
                for j in range(4):
                    mi = t.transpose(p_tb[:, 512 + j * C:512 + (j + 1) * C],
                                     qkvs2(c % 2)[:, j * C:(j + 1) * C], ibf[:])
            sT.inc(mi)
        t_kqt = sT.n

        @vec
        def _(v, c=c, own=own, t_kqt=t_kqt):
            v.wait_ge(sT.h, t_kqt)
            if own:
                sV.inc(v.tensor_copy(kqfm(c % 2), p_tb[:]))
            else:
                sV.inc(v.tensor_copy(kqfm(c % 2)[:, 0:512], p_tb[:, 0:512]))
            sn[("p2_kqfm", c)] = (sV.h, sV.n)
        s_kq = (sV.h, sV.n)

        @blk.tensor
        def _(t, c=c, own=own, s_kq=s_kq):
            t.wait_ge(s_kq[0], s_kq[1])
            if ("p2_kkfree", c - 1) in sn:
                sem, val = sn[("p2_kkfree", c - 1)]
                t.wait_ge(sem, val)
            mi = None
            for j in range(4):
                mi = t.matmul(p_kk[:, j * C:(j + 1) * C],
                              kqfm(c % 2)[:, j * C:(j + 1) * C],
                              kqfm(c % 2)[:, j * C:(j + 1) * C],
                              start=True, stop=True)
            sT.inc(mi)
            if own:
                for j in range(4):
                    mi = t.matmul(p_qk[:, j * C:(j + 1) * C],
                                  kqfm(c % 2)[:, j * C:(j + 1) * C],
                                  kqfm(c % 2)[:, 512 + j * C:512 + (j + 1) * C],
                                  start=True, stop=True)
                sT.inc(mi)
        t_kk = sT.n

        @vec
        def _(v, c=c, own=own):
            if ("p2_dmm", c - 1, 7) in sn:
                v.wait_ge(sT.h, sn[("p2_dmm", c - 1, 7)])
            mi = None
            for h in range(8):
                v.tensor_scalar_add(dpre[:, h * 384:h * 384 + 128],
                                    masks[:, 0:128], colsb[:, h:h + 1])
                mi = v.tensor_scalar_add(dpre[:, h * 384 + 128:h * 384 + 256],
                                         masks[:, 128:256],
                                         colsb[:, 8 + h:9 + h])
                if own:
                    mi = v.tensor_scalar_add(dpre[:, h * 384 + 256:h * 384 + 384],
                                             masks[:, 256:384],
                                             colsb[:, 16 + h:17 + h])
            sV.inc(mi)
        v_dpre = sV.n

        for hp in (0, 2, 4, 6):
            for h in (hp, hp + 1):
                j = h // 2

                @blk.tensor
                def _(t, c=c, h=h, own=own, v_dpre=v_dpre):
                    if h == 0:
                        t.wait_ge(sV.h, v_dpre)
                        if ("p2_post_free", c - 1) in sn:
                            sem, val = sn[("p2_post_free", c - 1)]
                            t.wait_ge(sem, val)
                    if h >= 2:
                        sem, val = sn[("p2_eexp", c, h - 2)]
                        t.wait_ge(sem, val)
                    t.transpose(p_r[h % 2][:, 0:128],
                                dpre[:, h * 384:h * 384 + 128], if32[:])
                    mi = t.transpose(p_r[h % 2][:, 128:256],
                                     dpre[:, h * 384 + 128:h * 384 + 256],
                                     if32[:])
                    if own:
                        mi = t.transpose(p_r[h % 2][:, 256:384],
                                         dpre[:, h * 384 + 256:h * 384 + 384],
                                         if32[:])
                    sT.inc(mi)
                    sn[("p2_dmm", c, h)] = sT.n

            @blk.scalar
            def _(sc, c=c, hp=hp, own=own, s_cols=s_cols):
                sc.wait_ge(sT.h, sn[("p2_dmm", c, hp + 1)])
                if hp == 0:
                    sc.wait_ge(sS.h, s_cols)
                    if ("p2_ntn", c - 1, 7) in sn:
                        sem, val = sn[("p2_ntn", c - 1, 7)]
                        sc.wait_ge(sem, val)
                else:
                    sem, val = sn[("p2_ntn", c, hp - 1)]
                    sc.wait_ge(sem, val)
                for h in (hp, hp + 1):
                    sc.activation(ebuf(h % 2)[:, 0:128], p_r[h % 2][:, 0:128],
                                  AF.Exp, bias=colsb[:, 8 + h:9 + h])
                    mi = sc.activation(ebuf(h % 2)[:, 128:256],
                                       p_r[h % 2][:, 128:256],
                                       AF.Exp, bias=colsb[:, h:h + 1])
                    if own:
                        mi = sc.activation(ebuf(h % 2)[:, 256:384],
                                           p_r[h % 2][:, 256:384],
                                           AF.Exp, bias=colsb[:, 24 + h:25 + h])
                    sS.inc(mi)
                    sn[("p2_eexp", c, h)] = (sS.h, sS.n)

            for h in (hp, hp + 1):
                j = h // 2

                @vec
                def _(v, c=c, h=h, j=j, own=own, t_kk=t_kk):
                    sem, val = sn[("p2_eexp", c, h)]
                    v.wait_ge(sem, val)
                    if h == 0:
                        v.wait_ge(sT.h, t_kk)
                    if ("p2_scan_mm", c - 1, h) in sn:
                        v.wait_ge(sT.h, sn[("p2_scan_mm", c - 1, h)])
                    v.scalar_tensor_tensor(ntbh(0, h),
                                           p_kk[:, j * C:(j + 1) * C],
                                           cst[:, 9:10],
                                           ebuf(h % 2)[:, 0:128],
                                           ALU.mult, ALU.mult)
                    mi = v.scalar_tensor_tensor(nbh(0, h),
                                                p_kk[:, j * C:(j + 1) * C],
                                                cst[:, 9:10],
                                                ebuf(h % 2)[:, 128:256],
                                                ALU.mult, ALU.mult)
                    if own:
                        mi = v.tensor_tensor(mtb[:, khd(h)],
                                             p_qk[:, j * C:(j + 1) * C],
                                             ebuf(h % 2)[:, 256:384], ALU.mult)
                    v.tensor_scalar_mul(rbb(0)[:, h * 256:h * 256 + 128],
                                        qkvs2(c % 2)[:, KEY + j * C:KEY + (j + 1) * C],
                                        colA[:, h:h + 1])
                    mi = v.tensor_scalar_mul(
                        rbb(0)[:, h * 256 + 128:(h + 1) * 256],
                        qkvs2(c % 2)[:, 2 * KEY + h * C:2 * KEY + (h + 1) * C],
                        colSB[:, h:h + 1])
                    sV.inc(mi)
                    sn[("p2_ntn", c, h)] = (sV.h, sV.n)

        for li, lv in enumerate(LVLS):
            src = li % 2
            dst = (li + 1) % 2
            for h in range(8):
                @blk.tensor
                def _(t, c=c, li=li, h=h, src=src):
                    if h < 2:
                        if li == 0:
                            sem, val = sn[("p2_ntn", c, h)]
                            t.wait_ge(sem, val)
                            sem, val = sn[("p2_eexp", c, 6 + h)]
                            t.wait_ge(sem, val)
                        elif li - 1 <= LSQ:
                            sem, val = sn[("p2_sq_ev", c, li - 1, h)]
                            t.wait_ge(sem, val)
                        else:
                            sem, val = sn[("p2_r_ev", c, li - 1, 6 + h)]
                            t.wait_ge(sem, val)
                    if h >= 2:
                        sem, val = sn[("p2_r_ev", c, li, h - 2)]
                        t.wait_ge(sem, val)
                    t.matmul(p_r[h % 2][:, 0:256], ntbh(src, h),
                             rbb(src)[:, h * 256:(h + 1) * 256],
                             start=True, stop=False)
                    sT.inc(t.matmul(p_r[h % 2][:, 0:256], ibf[:],
                                    rbb(src)[:, h * 256:(h + 1) * 256],
                                    start=False, stop=True))
                    sn[("p2_r_mm", c, li, h)] = sT.n

                @vec
                def _(v, c=c, li=li, h=h, dst=dst):
                    v.wait_ge(sT.h, sn[("p2_r_mm", c, li, h)])
                    sV.inc(v.tensor_copy(rbb(dst)[:, h * 256:(h + 1) * 256],
                                         p_r[h % 2][:, 0:256]))
                    sn[("p2_r_ev", c, li, h)] = (sV.h, sV.n)

            if li <= LSQ:
                last = li == LSQ
                for h in range(8):
                    @blk.tensor
                    def _(t, c=c, li=li, h=h, src=src, last=last):
                        if h < 2:
                            if li == 0:
                                if ("p2_scan_ev", c - 1, 7) in sn:
                                    sem, val = sn[("p2_scan_ev", c - 1, 7)]
                                    t.wait_ge(sem, val)
                            else:
                                sem, val = sn[("p2_sq_ev", c, li - 1, 6 + h)]
                                t.wait_ge(sem, val)
                        else:
                            sem, val = sn[("p2_sq_ev", c, li, h - 2)]
                            t.wait_ge(sem, val)
                        if not last:
                            t.matmul(p_sx[h % 2][:, 0:128], ntbh(src, h),
                                     nbh(src, h), start=True, stop=True)
                        sT.inc(t.matmul(p_sx[h % 2][:, 128:256],
                                        nbh(src, h),
                                        ntbh(src, h), start=True, stop=True))
                        sn[("p2_sq_mm", c, li, h)] = sT.n

                    @vec
                    def _(v, c=c, li=li, h=h, dst=dst, last=last):
                        v.wait_ge(sT.h, sn[("p2_sq_mm", c, li, h)])
                        if not last:
                            mi = v.tensor_copy(nnh(dst, h), p_sx[h % 2][:, 0:256])
                        else:
                            mi = v.tensor_copy(ntbh(dst, h),
                                               p_sx[h % 2][:, 128:256])
                        sV.inc(mi)
                        sn[("p2_sq_ev", c, li, h)] = (sV.h, sV.n)

        rfin = len(LVLS) % 2

        for h in range(8):
            j = h // 2

            @vec
            def _(v, c=c, h=h, j=j, rfin=rfin):
                sem, val = sn[("p2_r_ev", c, len(LVLS) - 1, h)]
                v.wait_ge(sem, val)
                if ("p2_scan_mm", c - 1, h) in sn:
                    v.wait_ge(sT.h, sn[("p2_scan_mm", c - 1, h)])
                v.tensor_scalar_mul(wu2[:, h * 256:h * 256 + 128],
                                    rbb(rfin)[:, h * 256:h * 256 + 128],
                                    colSB[:, h:h + 1])
                v.tensor_scalar_mul(wu2[:, h * 256 + 128:(h + 1) * 256],
                                    rbb(rfin)[:, h * 256 + 128:(h + 1) * 256],
                                    colSB[:, h:h + 1])
                sV.inc(v.tensor_scalar_mul(
                    kpp[:, khd(h)],
                    qkvs2(c % 2)[:, KEY + j * C:KEY + (j + 1) * C],
                    ecolK[:, h:h + 1]))
                sn[("p2_wu", c, h)] = sV.n

            if own:
                @blk.tensor
                def _(t, c=c, h=h):
                    t.wait_ge(sV.h, sn[("p2_wu", c, h)])
                    if h < 2:
                        sem, val = sn[("p2_sq_ev", c, LSQ, 6 + h)]
                        t.wait_ge(sem, val)
                    else:
                        sem, val = sn[("p2_scan_ev", c, h - 2)]
                        t.wait_ge(sem, val)
                    sT.inc(t.matmul(p_sx[h % 2][:, 0:128],
                                    wu2[:, h * 256:h * 256 + 128],
                                    mtb[:, khd(h)], start=True, stop=True))
                    sn[("p2_mumw", c, h)] = sT.n

                @vec
                def _(v, c=c, h=h):
                    v.wait_ge(sT.h, sn[("p2_mumw", c, h)])
                    sV.inc(v.tensor_scalar_mul(qtfm[:, khd(h)],
                                               p_sx[h % 2][:, 0:128],
                                               cst[:, 9:10]))
                    sn[("p2_muq", c, h)] = (sV.h, sV.n)

            @blk.tensor
            def _(t, c=c, h=h, own=own):
                if not own:
                    t.wait_ge(sV.h, sn[("p2_wu", c, h)])
                if h < 2:
                    sem, val = sn[("p2_r_ev", c, len(LVLS) - 1, 6 + h)]
                    t.wait_ge(sem, val)
                else:
                    sem, val = sn[("p2_ptz", c, h - 2)]
                    t.wait_ge(sem, val)
                t.matmul(p_r[h % 2][:, 0:128], wu2[:, h * 256:h * 256 + 128],
                         kpp[:, khd(h)], start=True, stop=True)
                sT.inc(t.matmul(p_r[h % 2][:, 128:256], kpp[:, khd(h)],
                                wu2[:, h * 256 + 128:(h + 1) * 256],
                                start=True, stop=True))
                sn[("p2_ptzmm", c, h)] = sT.n

            @vec
            def _(v, c=c, h=h):
                v.wait_ge(sT.h, sn[("p2_ptzmm", c, h)])
                v.scalar_tensor_tensor(ptb[:, khd(h)], ibf[:], ecb[:, h:h + 1],
                                       p_r[h % 2][:, 0:128], ALU.mult, ALU.subtract)
                sV.inc(v.tensor_copy(ztb[:, khd(h)], p_r[h % 2][:, 128:256]))
                sn[("p2_ptz", c, h)] = (sV.h, sV.n)

            scur = s_state[:, (c % 2) * VAL + h * DK:(c % 2) * VAL + (h + 1) * DK]
            snew = s_state[:, ((c + 1) % 2) * VAL + h * DK:((c + 1) % 2) * VAL + (h + 1) * DK]

            @blk.tensor
            def _(t, c=c, h=h, j=j, own=own, scur=scur):
                sem, val = sn[("p2_ptz", c, h)]
                t.wait_ge(sem, val)
                if own:
                    sem2, v2 = sn[("p2_muq", c, h)]
                    t.wait_ge(sem2, v2)
                if h >= 2:
                    sem3, v3 = sn[("p2_scan_ev", c, h - 2)]
                    t.wait_ge(sem3, v3)
                if own:
                    t.matmul(p_sx[h % 2][:, 128:256], mtb[:, khd(h)],
                             wu2[:, h * 256 + 128:(h + 1) * 256],
                             start=True, stop=False)
                    t.matmul(p_sx[h % 2][:, 128:256], qtfm[:, khd(h)], scur,
                             start=False, stop=True)
                    t.matmul(p_sx[h % 2][:, 0:128],
                             kqfm(c % 2)[:, 512 + j * C:512 + (j + 1) * C],
                             scur, start=True, stop=True)
                t.matmul(p_sx[h % 2][:, 256:384], ptb[:, khd(h)], scur,
                         start=True, stop=False)
                sT.inc(t.matmul(p_sx[h % 2][:, 256:384], ibf[:], ztb[:, khd(h)],
                                start=False, stop=True))
                sn[("p2_scan_mm", c, h)] = sT.n

            @vec
            def _(v, c=c, h=h, own=own, snew=snew):
                v.wait_ge(sT.h, sn[("p2_scan_mm", c, h)])
                if own:
                    v.tensor_scalar_mul(otile[:, khd(h)], p_sx[h % 2][:, 0:128],
                                        colQ[:, h:h + 1])
                    v.tensor_tensor(otile[:, khd(h)], otile[:, khd(h)],
                                    p_sx[h % 2][:, 128:256], ALU.add)
                sV.inc(v.tensor_copy(snew, p_sx[h % 2][:, 256:384]))
                sn[("p2_scan_ev", c, h)] = (sV.h, sV.n)
        sn[("p2_post_free", c)] = sn[("p2_scan_ev", c, 7)]

        if not own:
            sn[("p2_done", c)] = sn[("p2_scan_ev", c, 7)]
            sn[("p2_cols_free", c)] = sn[("p2_eexp", c, 7)]
            sn[("p2_kkfree", c)] = sn[("p2_ntn", c, 7)]
            sn[("p2_ptb_free", c)] = sn[("p2_kqfm", c)]
            continue

        if dbg and c == NPRE:
            @blk.sync
            def _(s, c=c):
                sem, val = sn[("p2_scan_ev", c, 7)]
                s.wait_ge(sem, val)
                sL.inc(s.dma_start(out=sp_dbgf.ap()[:, 0:64], in_=colsb[:]), 16)
                sL.inc(s.dma_start(out=sp_dbgf.ap()[:, 64:72], in_=gtile[:]), 16)
                sL.inc(s.dma_start(out=sp_dbgf.ap()[:, 72:80], in_=colA[:]), 16)
                sL.inc(s.dma_start(out=sp_dbgf.ap()[:, 80:88], in_=colSB[:]), 16)
                sL.inc(s.dma_start(out=sp_dbgf.ap()[:, 88:96], in_=colQ[:]), 16)
                sL.inc(s.dma_start(out=sp_dbgf.ap()[:, 96:104], in_=ecolK[:]), 16)
                sL.inc(s.dma_start(out=sp_dbgf.ap()[:, 104:112], in_=ecb[:]), 16)
                sL.inc(s.dma_start(out=sp_dbgf.ap()[:, 112:120], in_=gcb[:]), 16)
                sL.inc(s.dma_start(out=sp_dbgf.ap()[:, 120:128], in_=expA_b[:]), 16)
                sL.inc(s.dma_start(out=sp_dbgf.ap()[:, 128:136], in_=dt_b[:]), 16)
                sL.inc(s.dma_start(out=sp_dbgf.ap()[:, 136:152],
                                   in_=col_ba[:, c * 16:(c + 1) * 16]), 16)
                sL.inc(s.dma_start(out=sp_dbgf.ap()[:, 152:160],
                                   in_=col_ssq[:, c * 8:(c + 1) * 8]), 16)
                sL.inc(s.dma_start(out=sp_dbgb.ap()[:, 0:256], in_=wu2[:, 0:256]), 16)
                sL.inc(s.dma_start(out=sp_dbgb.ap()[:, 256:384], in_=kpp[:, 0:128]), 16)
                sL.inc(s.dma_start(out=sp_dbgb.ap()[:, 384:512], in_=mtb[:, 0:128]), 16)
                sL.inc(s.dma_start(out=sp_dbgb.ap()[:, 512:640], in_=mub[:, 0:128]), 16)
                sL.inc(s.dma_start(out=sp_dbgb.ap()[:, 640:768], in_=ztb[:, 0:128]), 16)
                sL.inc(s.dma_start(out=sp_dbgb.ap()[:, 768:896], in_=ptb[:, 0:128]), 16)
                sL.inc(s.dma_start(out=sp_dbgb.ap()[:, 896:1024], in_=qtfm[:, 0:128]), 16)
                sL.inc(s.dma_start(out=sp_dbgb.ap()[:, 1024:1280], in_=rbb(1)[:, 0:256]), 16)
                sL.inc(s.dma_start(out=sp_dbgb.ap()[:, 1280:1408], in_=ntbh(0, 0)), 16)
                sL.inc(s.dma_start(out=sp_dbgb.ap()[:, 1408:1536],
                                   in_=s_state[:, ((c + 1) % 2) * VAL:((c + 1) % 2) * VAL + 128]), 16)
                sn[("p2_dump", c)] = sL.n

        @blk.scalar
        def _(sc, c=c):
            sem, val = sn[("p2_scan_ev", c, 7)]
            sc.wait_ge(sem, val)
            for h in range(8):
                sc.activation(sqscr[:, 0:128], otile[:, khd(h)], AF.Square,
                              accum_out=rs8[:, h:h + 1])
            sc.copy(rm8[:, 4:5], rm8[:, 4:5])
            sc.activation(rs8[:], rs8[:], AF.Ln, bias=EPS, scale=1.0 / DV)
            sc.copy(rm8[:, 4:5], rm8[:, 4:5])
            sS.inc(sc.activation(rs8[:], rs8[:], AF.Exp, scale=-0.5))
        s_rs = sS.n

        @vec
        def _(v, c=c, s_rs=s_rs, p2_gld=p2_gld):
            v.wait_ge(sS.h, s_rs)
            v.wait_ge(sG.h, p2_gld)
            mi = None
            for h in range(8):
                mi = v.scalar_tensor_tensor(og[:, khd(h)], otile[:, khd(h)],
                                            rs8[:, h:h + 1],
                                            szf(c % 2)[:, khd(h)],
                                            ALU.mult, ALU.mult)
            sV.inc(mi)
            sn[("p2_og", c)] = sV.n
        v_og = sV.n

        @blk.tensor
        def _(t, c=c, v_og=v_og):
            t.wait_ge(sV.h, v_og)
            mi = None
            for k in range(8):
                mi = t.transpose(p_tb[:, k * C:(k + 1) * C], og[:, k * C:(k + 1) * C],
                                 ibf[:])
            sT.inc(mi)
        t_ogt = sT.n

        @vec
        def _(v, c=c, t_ogt=t_ogt):
            v.wait_ge(sT.h, t_ogt)
            sV.inc(v.tensor_scalar_mul(ogfm[:], p_tb[:], gnw_c[:, 0:1]))
        v_ogfm = sV.n

        @blk.tensor
        def _(t, c=c, v_ogfm=v_ogfm):
            t.wait_ge(sV.h, v_ogfm)
            if c == NPRE:
                t.wait_ge(sG.h, gl_wout)
            for dh in range(2):
                ps = p_kk if dh == 0 else p_qk
                mi = None
                for d in range(8):
                    mi = t.matmul(ps[:], ogfm[:, d * C:(d + 1) * C],
                                  wout_bf[:, d * HID + dh * 512:d * HID + (dh + 1) * 512],
                                  start=(d == 0), stop=(d == 7))
                sT.inc(mi)
        t_wo = sT.n

        @vec
        def _(v, c=c, t_wo=t_wo):
            v.wait_ge(sT.h, t_wo)
            v.tensor_scalar(xnsb[:], x2t(c % 2), col_m[:, c:c + 1],
                            col_rstd[:, c:c + 1], ALU.subtract, ALU.mult)
            v.tensor_tensor(hsb[:, 0:512], p_kk[:], xnsb[:, 0:512], ALU.add)
            sV.inc(v.tensor_tensor(hsb[:, 512:1024], p_qk[:], xnsb[:, 512:1024], ALU.add))
        v_h = sV.n
        sn[("p2_kkfree", c)] = (sV.h, v_h)

        @blk.scalar
        def _(sc, c=c, v_h=v_h):
            sc.wait_ge(sV.h, v_h)
            sc.activation(sqscr[:], hsb[:, 0:1024], AF.Square,
                          accum_out=rm8[:, 0:1])
            sc.copy(rm8[:, 3:4], rm8[:, 3:4])
            sc.activation(rm8[:, 1:2], rm8[:, 0:1], AF.Ln, bias=EPS, scale=1.0 / HID)
            sc.copy(rm8[:, 2:3], rm8[:, 2:3])
            sS.inc(sc.activation(rm8[:, 1:2], rm8[:, 1:2], AF.Exp, scale=-0.5))
        s_r2 = sS.n

        @vec
        def _(v, c=c, s_r2=s_r2):
            v.wait_ge(sS.h, s_r2)
            sV.inc(v.tensor_scalar_mul(x2rt[:], hsb[:], rm8[:, 1:2]))
        v_x2r = sV.n

        @blk.tensor
        def _(t, c=c, v_x2r=v_x2r):
            t.wait_ge(sV.h, v_x2r)
            mi = None
            for k in range(8):
                mi = t.transpose(p_tb[:, k * C:(k + 1) * C], x2rt[:, k * C:(k + 1) * C],
                                 ibf[:])
            sT.inc(mi)
        t_x2t = sT.n

        @vec
        def _(v, c=c, t_x2t=t_x2t):
            v.wait_ge(sT.h, t_x2t)
            if ("p2_osp", c - 1) in sn:
                v.wait_ge(sL.h, sn[("p2_osp", c - 1)])
            mi = None
            for k in range(8):
                mi = v.tensor_scalar_mul(x2rfm[:, k * C:(k + 1) * C],
                                         p_tb[:, k * C:(k + 1) * C],
                                         ln12[:, 8 + k:9 + k])
            sV.inc(mi)
            sn[("p2_ptb_free", c)] = (sV.h, sV.n)
        v_x2fm = sV.n

        @blk.tensor
        def _(t, c=c, v_x2fm=v_x2fm):
            t.wait_ge(sV.h, v_x2fm)
            mi = None
            for d in range(8):
                mi = t.matmul(p_sm[:, 16:24], x2rfm[:, d * C:(d + 1) * C],
                              wrt_bf[:, d * 8:(d + 1) * 8], start=(d == 0), stop=(d == 7))
            sT.inc(mi)
        t_rt = sT.n

        @vec
        def _(v, c=c, t_rt=t_rt):
            v.wait_ge(sT.h, t_rt)
            v.reduce_max(smx[:, 0:1], p_sm[:, 16:24], axis=AX.X)
            v.reduce_max(smx[:, 24:25], col_fflog[:, c * 8:(c + 1) * 8], axis=AX.X)
            v.tensor_scalar_sub(smx[:, 8:16], p_sm[:, 16:24], smx[:, 0:1])
            sV.inc(v.tensor_scalar_sub(smx[:, 32:40], col_fflog[:, c * 8:(c + 1) * 8],
                                       smx[:, 24:25]))

        @blk.scalar
        def _(sc, c=c):
            sc.wait_ge(sV.h, sV.n)
            sc.activation(smx[:, 8:16], smx[:, 8:16], AF.Exp)
            sS.inc(sc.activation(smx[:, 32:40], smx[:, 32:40], AF.Exp))

        @vec
        def _(v, c=c, co=co):
            v.wait_ge(sS.h, sS.n)
            v.reduce_sum(smx[:, 1:2], smx[:, 8:16], axis=AX.X)
            v.reduce_sum(smx[:, 25:26], smx[:, 32:40], axis=AX.X)
            v.reciprocal(smx[:, 2:3], smx[:, 1:2])
            v.reciprocal(smx[:, 26:27], smx[:, 25:26])
            v.tensor_scalar_mul(smx[:, 8:16], smx[:, 8:16], smx[:, 2:3])
            v.tensor_scalar_mul(smx[:, 32:40], smx[:, 32:40], smx[:, 26:27])
            v.reduce_max(smx[:, 0:1], smx[:, 8:16], axis=AX.X)
            v.reduce_max(smx[:, 24:25], smx[:, 32:40], axis=AX.X)
            v.tensor_scalar(smx[:, 16:24], smx[:, 8:16], smx[:, 0:1], None, ALU.is_equal)
            v.tensor_scalar(smx[:, 40:48], smx[:, 32:40], smx[:, 24:25], None,
                            ALU.is_equal)
            v.scalar_tensor_tensor(smx[:, 48:56], smx[:, 16:24], cst[:, 8:9],
                                   smx[:, 8:16], ALU.mult, ALU.add)
            v.scalar_tensor_tensor(smx[:, 56:64], smx[:, 40:48], cst[:, 8:9],
                                   smx[:, 32:40], ALU.mult, ALU.add)
            v.reduce_max(smx[:, 3:4], smx[:, 48:56], axis=AX.X)
            v.reduce_max(smx[:, 27:28], smx[:, 56:64], axis=AX.X)
            v.tensor_tensor(smx[:, 4:5], smx[:, 0:1], smx[:, 3:4], ALU.add)
            v.tensor_tensor(smx[:, 28:29], smx[:, 24:25], col_ir1[:, c:c + 1], ALU.mult)
            v.reciprocal(smx[:, 5:6], smx[:, 4:5])
            v.tensor_tensor(smx[:, 29:30], smx[:, 27:28], col_ir1[:, c:c + 1], ALU.mult)
            v.tensor_tensor(smx[:, 6:7], smx[:, 0:1], smx[:, 5:6], ALU.mult)
            v.tensor_tensor(smx[:, 7:8], smx[:, 3:4], smx[:, 5:6], ALU.mult)
            v.tensor_scalar(smx[:, 16:24], smx[:, 8:16], smx[:, 0:1], smx[:, 6:7],
                            ALU.is_equal, ALU.mult)
            v.tensor_scalar(smx[:, 48:56], smx[:, 8:16], smx[:, 3:4], smx[:, 7:8],
                            ALU.is_equal, ALU.mult)
            v.tensor_scalar(smx[:, 40:48], smx[:, 32:40], smx[:, 24:25], smx[:, 28:29],
                            ALU.is_equal, ALU.mult)
            v.tensor_scalar(smx[:, 56:64], smx[:, 32:40], smx[:, 27:28], smx[:, 29:30],
                            ALU.is_equal, ALU.mult)
            v.tensor_tensor(col_route[:, co * 8:(co + 1) * 8], smx[:, 16:24],
                            smx[:, 48:56], ALU.add)
            sV.inc(v.tensor_tensor(col_sffn[:, co * 8:(co + 1) * 8], smx[:, 40:48],
                                   smx[:, 56:64], ALU.add))
        v_rt = sV.n
        sn[("p2_cols_free", c)] = sn[("p2_eexp", c, 7)]

        @vec
        def _(v, c=c):
            if ("p2_osp", c - 1) in sn:
                v.wait_ge(sL.h, sn[("p2_osp", c - 1)])
            sV.inc(v.tensor_tensor(outp[:], x2t(c % 2), hsb[:], ALU.add))
        v_outp = sV.n

        @blk.sync
        def _(s, c=c, co=co, v_outp=v_outp):
            s.wait_ge(sV.h, v_outp)
            sL.inc(s.dma_start(out=sp_outp.ap()[co], in_=outp[:]), 16)
            sL.inc(s.dma_start(out=sp_x2rfm.ap()[co], in_=x2rfm[:]), 16)
            if dbg:
                sL.inc(s.dma_start(out=sp_otile.ap()[co], in_=otile[:]), 16)
            sn[("p2_osp", c)] = sL.n
        sn[("p2_done", c)] = (sL.h, sn[("p2_osp", c)])

    p2s.close()

    phase_gate(blk)
    p3s = ExitStack()
    xin2 = p3s.enter_context(nc.sbuf_tensor("xin2", [C, 4 * HID], BF16))
    xin1 = p3s.enter_context(nc.sbuf_tensor("xin1", [C, 4 * HID], BF16))
    op4 = p3s.enter_context(nc.sbuf_tensor("op4", [C, 4 * HID], F32))
    wgu_t = p3s.enter_context(nc.sbuf_tensor("wgu", [C, 4 * 8 * INTER], BF16))
    wgu = lambda i, k: wgu_t[:, (i * 2 + k) * 8 * INTER:(i * 2 + k + 1) * 8 * INTER]
    wdn_t = p3s.enter_context(nc.sbuf_tensor("wdn", [C, 2 * 4 * HID], BF16))
    wdn = lambda i: wdn_t[:, i * 4 * HID:(i + 1) * 4 * HID]
    htm_t = p3s.enter_context(nc.sbuf_tensor("htm", [C, 4 * INTER], BF16))
    htm = lambda i: htm_t[:, i * INTER:(i + 1) * INTER]
    sgl_t = p3s.enter_context(nc.sbuf_tensor("sgl", [C, 2 * INTER], F32))
    sgl = lambda i: sgl_t[:, i * INTER:(i + 1) * INTER]
    hst_t = p3s.enter_context(nc.sbuf_tensor("hst", [C, 16 * 4 * INTER], BF16))
    hst = lambda i: hst_t[:, i * 4 * INTER:(i + 1) * 4 * INTER]
    ost_t = p3s.enter_context(nc.sbuf_tensor("ost", [C, 2 * HID], F32))
    ost = lambda i: ost_t[:, i * HID:(i + 1) * HID]

    for ti in range(8):
        @blk.sync
        def _(s, ti=ti):
            if ti == 0:
                s.wait_ge(sO.h, mst["st"][0] or 0)
                s.wait_ge(sO.h, mst["st"][1] or 0)
            if ti >= 1:
                sem, val = sn[("p3_done", ti - 1)]
                s.wait_ge(sem, val)
            sem, val = sn[("p2_done", NPRE + 4 * ti + 3)]
            s.wait_ge(sem, val)
            sL.inc(s.dma_start(out=xin2[:].rearrange("p (c f) -> p c f", c=4),
                               in_=sp_x2rfm.ap()[4 * ti:4 * ti + 4]
                                   .rearrange("c p f -> p c f")), 16)
            sL.inc(s.dma_start(out=xin1[:].rearrange("p (c f) -> p c f", c=4),
                               in_=sp_xnrfm.ap()[NPRE + 4 * ti:NPRE + 4 * ti + 4]
                                   .rearrange("c p f -> p c f")), 16)
            sL.inc(s.dma_start(out=op4[:].rearrange("p (c f) -> p c f", c=4),
                               in_=sp_outp.ap()[4 * ti:4 * ti + 4]
                                   .rearrange("c p f -> p c f")), 16)
        p3_ld = sL.n

        hps = ExitStack()
        pg = [hps.enter_context(nc.psum_tensor("p3_pg%d_%d" % (ti, i),
                                                [C, INTER], F32)) for i in range(3)]
        pu = [hps.enter_context(nc.psum_tensor("p3_pu%d_%d" % (ti, i),
                                                [C, INTER], F32)) for i in range(3)]
        pt3 = [hps.enter_context(nc.psum_tensor("p3_pt%d_%d" % (ti, i),
                                                 [C, INTER], BF16)) for i in range(2)]

        p3pend = []
        for me in range(16):
            mo, e = me // 8, me % 8
            gu_ti = (0, 1) if mo == 0 else (2, 3)
            xin = xin2 if mo == 0 else xin1
            scol = col_route if mo == 0 else col_sffn

            @blk.sync
            def _(s, ti=ti, me=me, e=e, gu_ti=gu_ti):
                if me >= 2:
                    sem, val = sn[("p3_gu_mm", ti, me - 2, 3)]
                    s.wait_ge(sem, val)
                elif ti >= 1 and me < 2:
                    sem, val = sn[("p3_gu_mm", ti - 1, 14 + me, 3)]
                    s.wait_ge(sem, val)
                sL.inc(s.dma_start(out=wgu(me % 2, 0).rearrange("p (c f) -> p c f", c=8),
                                   in_=l_gu.ap()[gu_ti[0], e]), 16)
                sL.inc(s.dma_start(out=wgu(me % 2, 1).rearrange("p (c f) -> p c f", c=8),
                                   in_=l_gu.ap()[gu_ti[1], e]), 16)
            p3_wld = sL.n

            for ccp in (0, 2):
                for cc in (ccp, ccp + 1):
                    sq = (ti * 16 + me) * 4 + cc

                    @blk.tensor
                    def _(t, ti=ti, me=me, cc=cc, sq=sq, xin=xin,
                          p3_ld=p3_ld, p3_wld=p3_wld):
                        t.wait_ge(sL.h, max(p3_ld, p3_wld))
                        if ("p3_huq", sq - 3) in sn:
                            sem, val = sn[("p3_huq", sq - 3)]
                            t.wait_ge(sem, val)
                        if me == 0 and cc == 0 and ti >= 1:
                            sem, val = sn[("p3_oev", ti - 1)]
                            t.wait_ge(sem, val)
                        mi = None
                        for d in range(8):
                            mi = t.matmul(pg[sq % 3][:],
                                          xin[:, cc * HID + d * C:cc * HID + (d + 1) * C],
                                          wgu(me % 2, 0)[:, d * INTER:(d + 1) * INTER],
                                          start=(d == 0), stop=(d == 7))
                        for d in range(8):
                            mi = t.matmul(pu[sq % 3][:],
                                          xin[:, cc * HID + d * C:cc * HID + (d + 1) * C],
                                          wgu(me % 2, 1)[:, d * INTER:(d + 1) * INTER],
                                          start=(d == 0), stop=(d == 7))
                        sT.inc(mi)
                        sn[("p3_gu_mm", ti, me, cc)] = (sT.h, sT.n)

                @blk.scalar
                def _(sc, ti=ti, me=me, mo=mo, e=e, ccp=ccp, scol=scol):
                    sem, val = sn[("p3_gu_mm", ti, me, ccp + 1)]
                    sc.wait_ge(sem, val)
                    for cc in (ccp, ccp + 1):
                        sq2 = (ti * 16 + me) * 4 + cc
                        co = 4 * ti + cc
                        if mo == 0:
                            sS.inc(sc.activation(sgl(cc % 2)[:], pg[sq2 % 3][:],
                                                 AF.Silu))
                        else:
                            sS.inc(sc.activation(sgl(cc % 2)[:], pg[sq2 % 3][:],
                                                 AF.Silu,
                                                 scale=scol[:, co * 8 + e:co * 8 + e + 1]))
                        sn[("p3_sg", ti, me, cc)] = sS.n

                for cc in (ccp, ccp + 1):
                    sq = (ti * 16 + me) * 4 + cc

                    @vec
                    def _(v, ti=ti, me=me, mo=mo, e=e, cc=cc, sq=sq, scol=scol):
                        v.wait_ge(sS.h, sn[("p3_sg", ti, me, cc)])
                        co = 4 * ti + cc
                        sV.inc(v.scalar_tensor_tensor(htm(sq % 4)[:], pu[sq % 3][:],
                                                      scol[:, co * 8 + e:co * 8 + e + 1],
                                                      sgl(cc % 2)[:],
                                                      ALU.mult, ALU.mult))
                        sn[("p3_hu", ti, me, cc)] = (sV.h, sV.n)
                        sn[("p3_huq", sq)] = (sV.h, sV.n)

                for (pme, pcc) in p3pend:
                    psq = (ti * 16 + pme) * 4 + pcc

                    @blk.tensor
                    def _(t, ti=ti, me=pme, cc=pcc, sq=psq):
                        sem, val = sn[("p3_hu", ti, me, cc)]
                        t.wait_ge(sem, val)
                        if ("p3_hevq", sq - 2) in sn:
                            sem2, v2 = sn[("p3_hevq", sq - 2)]
                            t.wait_ge(sem2, v2)
                        mi = None
                        for ii in range(4):
                            mi = t.transpose(pt3[cc % 2][:, ii * C:(ii + 1) * C],
                                             htm(sq % 4)[:, ii * C:(ii + 1) * C],
                                             ibf[:])
                        sT.inc(mi)
                        sn[("p3_htr", ti, me, cc)] = sT.n

                    @vec
                    def _(v, ti=ti, me=pme, cc=pcc, sq=psq):
                        v.wait_ge(sT.h, sn[("p3_htr", ti, me, cc)])
                        sV.inc(v.tensor_copy(hst(me)[:, cc * 512:(cc + 1) * 512],
                                             pt3[cc % 2][:]))
                        sn[("p3_hev", ti, me, cc)] = (sV.h, sV.n)
                        sn[("p3_hevq", sq)] = (sV.h, sV.n)
                p3pend[:] = [(me, ccp), (me, ccp + 1)]

        for (pme, pcc) in p3pend:
            psq = (ti * 16 + pme) * 4 + pcc

            @blk.tensor
            def _(t, ti=ti, me=pme, cc=pcc, sq=psq):
                sem, val = sn[("p3_hu", ti, me, cc)]
                t.wait_ge(sem, val)
                if ("p3_hevq", sq - 2) in sn:
                    sem2, v2 = sn[("p3_hevq", sq - 2)]
                    t.wait_ge(sem2, v2)
                mi = None
                for ii in range(4):
                    mi = t.transpose(pt3[cc % 2][:, ii * C:(ii + 1) * C],
                                     htm(sq % 4)[:, ii * C:(ii + 1) * C],
                                     ibf[:])
                sT.inc(mi)
                sn[("p3_htr", ti, me, cc)] = sT.n

            @vec
            def _(v, ti=ti, me=pme, cc=pcc, sq=psq):
                v.wait_ge(sT.h, sn[("p3_htr", ti, me, cc)])
                sV.inc(v.tensor_copy(hst(me)[:, cc * 512:(cc + 1) * 512],
                                     pt3[cc % 2][:]))
                sn[("p3_hev", ti, me, cc)] = (sV.h, sV.n)
                sn[("p3_hevq", sq)] = (sV.h, sV.n)

        hps.close()

        dps = ExitStack()
        pd = [dps.enter_context(nc.psum_tensor("p3_pd%d_%d" % (ti, i), [C, 512], F32))
              for i in range(8)]

        for me in range(16):
            mo, e = me // 8, me % 8

            @blk.sync
            def _(s, ti=ti, me=me, mo=mo, e=e):
                if me >= 2:
                    s.wait_ge(sT.h, sn[("p3_dn_mm", ti, me - 2)])
                elif ti >= 1 and me < 2:
                    s.wait_ge(sT.h, sn[("p3_dn_mm", ti - 1, 14 + me)])
                sL.inc(s.dma_start(out=wdn(me % 2).rearrange("p (c f) -> p c f", c=4),
                                   in_=l_dn.ap()[mo, e]), 16)
            p3_dld = sL.n

            @blk.tensor
            def _(t, ti=ti, me=me, p3_dld=p3_dld):
                t.wait_ge(sL.h, p3_dld)
                sem, val = sn[("p3_hev", ti, me, 3)]
                t.wait_ge(sem, val)
                if me == 0 and ti >= 1:
                    sem2, v2 = sn[("p3_oev", ti - 1)]
                    t.wait_ge(sem2, v2)
                mi = None
                for cc in range(4):
                    for dh in range(2):
                        for ii in range(4):
                            mi = t.matmul(
                                pd[cc * 2 + dh][:],
                                hst(me)[:, cc * 512 + ii * C:cc * 512 + (ii + 1) * C],
                                wdn(me % 2)[:, ii * HID + dh * 512:
                                                 ii * HID + (dh + 1) * 512],
                                start=(me == 0 and ii == 0),
                                stop=(me == 15 and ii == 3))
                sT.inc(mi)
                sn[("p3_dn_mm", ti, me)] = sT.n

        for cc in range(4):
            @vec
            def _(v, ti=ti, cc=cc):
                v.wait_ge(sT.h, sn[("p3_dn_mm", ti, 15)])
                if ("p3_ost", ti, cc - 2) in sn:
                    v.wait_ge(sL.h, sn[("p3_ost", ti, cc - 2)])
                elif cc < 2 and ti >= 1 and ("p3_ost", ti - 1, cc + 2) in sn:
                    v.wait_ge(sL.h, sn[("p3_ost", ti - 1, cc + 2)])
                mi = None
                for dh in range(2):
                    mi = v.tensor_tensor(
                        ost(cc % 2)[:, dh * 512:(dh + 1) * 512],
                        pd[cc * 2 + dh][:],
                        op4[:, cc * HID + dh * 512:cc * HID + (dh + 1) * 512], ALU.add)
                sV.inc(mi)
                sn[("p3_oevc", ti, cc)] = sV.n

            @blk.sync
            def _(s, ti=ti, cc=cc):
                s.wait_ge(sV.h, sn[("p3_oevc", ti, cc)])
                co = 4 * ti + cc
                sL.inc(s.dma_start(out=out_h.ap()[co * C:(co + 1) * C, :],
                                   in_=ost(cc % 2)), 16)
                sn[("p3_ost", ti, cc)] = sL.n

        sn[("p3_oev", ti)] = (sV.h, sn[("p3_oevc", ti, 3)])
        dps.close()
        sn[("p3_done", ti)] = (sL.h, sn[("p3_ost", ti, 3)])

    p3s.close()

    @blk.gpsimd
    def _(g):
        g.wait_ge(sO.h, sO.n)
        g.wait_ge(sL.h, sL.n)

    est.close()
    return nc


_CACHE = {}


def kernel(**inputs):
    x = np.ascontiguousarray(np.asarray(inputs["L0_attn_hidden_states"], np.float32))
    B, S, _ = x.shape
    Q = S // 4
    wmap = {
        "w_ln1": "L0_attn_w_ln1", "w_ln2": "L0_attn_w_ln2",
        "W_qkvz": "L0_attn_W_qkvz", "W_ba": "L0_attn_W_ba",
        "conv_w": "L0_attn_conv_w", "dt_bias": "L0_attn_dt_bias",
        "A_log": "L0_attn_A_log", "gate_norm_w": "L0_attn_gate_norm_w",
        "W_out": "L0_attn_W_out", "W_router": "L0_attn_W_router",
        "W_gate_e": "L0_attn_W_gate_e", "W_up_e": "L0_attn_W_up_e",
        "W_down_e": "L0_attn_W_down_e", "fgw": "L0_ffn_gate_w",
        "w1": "L0_ffn_w1", "w2": "L0_ffn_w2", "w3": "L0_ffn_w3",
    }
    ws = {k: np.ascontiguousarray(np.asarray(inputs[v], np.float32))
          for k, v in wmap.items()}
    if "nc" not in _CACHE:
        _CACHE["nc"] = build_program()
    nc = _CACHE["nc"]
    pad = NPRE * C
    in_maps = []
    for core in range(8):
        b, q = core // 4, core % 4
        lo = q * Q - pad
        if lo < 0:
            xs = np.concatenate([np.zeros((-lo, HID), np.float32),
                                 x[b, :q * Q + Q]], 0)
        else:
            xs = x[b, lo:(q + 1) * Q]
        m = {"x": np.ascontiguousarray(xs)}
        m.update(ws)
        in_maps.append(m)
    res = run_bass_kernel_spmd(nc, in_maps, core_ids=list(range(8)))
    halves = [np.concatenate([res.results[b * 4 + q]["out"] for q in range(4)], 0)
              for b in range(2)]
    return np.stack(halves, 0)
